# version 03
# baseline (speedup 1.0000x reference)
import numpy as np
from contextlib import ExitStack
import concourse.bass as bass
import concourse.mybir as mybir

F32 = mybir.dt.float32
BF16 = mybir.dt.bfloat16
AF = mybir.ActivationFunctionType
ALU = mybir.AluOpType

T = 256
SUB = T // 128
PC = 8
N = 16
NPG = 1
NGRP = N // NPG
D = 1024
EPS = 1e-5
GROUPS = [[0, 1], [2, 3], [4, 5], [6, 7]]
ENGS = ("sync", "scalar", "vector", "tensor", "gpsimd")


class Sem:
    def __init__(self, h):
        self.h = h
        self.count = 0


class Prog:
    def __init__(self):
        self.ops = {k: [] for k in ENGS}

    def emit(self, eng, fn, waits=(), inc=None, inck=1):
        ws = [(s.h, v) for s, v in waits if v > 0]
        if inc is not None:
            inc.count += inck
        self.ops[eng].append((fn, ws, (inc.h if inc else None), inck))

    def run(self, eng, e):
        for fn, waits, inch, inck in self.ops[eng]:
            for h, v in waits:
                e.wait_ge(h, v)
            ins = fn(e)
            if inch is not None:
                ins.then_inc(inch, inck)


def build(NT, debug=False):
    L = NT * T
    nc = bass.Bass("TRN2", target_bir_lowering=False)

    hid_d = nc.dram_tensor("hid", [L, D], F32, kind="ExternalInput").ap()
    w1_d = nc.dram_tensor("w1", [16, 128, 1024], F32, kind="ExternalInput").ap()
    cw_d = nc.dram_tensor("cw", [PC, 128, 4], F32, kind="ExternalInput").ap()
    cb_d = nc.dram_tensor("cb", [PC, 128, 1], F32, kind="ExternalInput").ap()
    dv_d = nc.dram_tensor("dv", [PC, 128, 1], F32, kind="ExternalInput").ap()
    dtb_d = nc.dram_tensor("dtb", [PC, 128, 1], F32, kind="ExternalInput").ap()
    xpt_d = nc.dram_tensor("xpt", [128, 768], F32, kind="ExternalInput").ap()
    dtht_d = nc.dram_tensor("dtht", [64, 1024], F32, kind="ExternalInput").ap()
    owt_d = nc.dram_tensor("owt", [16, 128, 1024], F32, kind="ExternalInput").ap()
    fwt_d = nc.dram_tensor("fwt", [8, 128, 1024], F32, kind="ExternalInput").ap()
    out_d = nc.dram_tensor("out", [L, D], F32, kind="ExternalOutput").ap()

    owbf_d = nc.dram_tensor("owbf", [16, 128, 1024], BF16, kind="Internal").ap()
    w1bf_d = nc.dram_tensor("w1bf", [16, 128, 1024], BF16, kind="Internal").ap()
    ffqd = nc.dram_tensor("ffqd", [8, 128, 1024], BF16, kind="Internal").ap()
    repsrc_d = [nc.dram_tensor(f"repsrc{i}", [49 * T], F32, kind="Internal").ap()
                for i in range(2)]
    YLEN = 8 * 128 * T // 2
    XLEN = 96 * T
    CLEN = YLEN + XLEN
    ccin_d = [nc.dram_tensor(f"ccin{i}", [CLEN], F32, kind="Internal").ap()
              for i in range(2)]
    ccout_d = [nc.dram_tensor(f"ccout{i}", [2, CLEN], F32, kind="Internal").ap()
               for i in range(2)]

    ctx = ExitStack()
    with ctx:
        def sbt(name, shape, dt=F32):
            return ctx.enter_context(nc.sbuf_tensor(name, shape, dt))

        def psum(name):
            return ctx.enter_context(nc.psum_tensor(name, [128, 512], F32))

        xpt_bf = sbt("xpt_bf", [128, 768], BF16)
        dtht_bf = sbt("dtht_bf", [64, 1024], BF16)
        ow_buf = [sbt(f"owb{i}", [128, 1024], BF16) for i in range(2)]
        w1_buf = [sbt(f"w1b{i}", [128, 1024], BF16) for i in range(2)]
        cw_sb = sbt("cw_sb", [128, PC * 4])
        cb_sb = sbt("cb_sb", [128, PC])
        dv_sb = sbt("dv_sb", [128, PC])
        ndtb_sb = sbt("ndtb_sb", [128, PC])
        ones16 = sbt("ones16", [16, 1])
        onesc = sbt("onesc", [128, 1])
        onesr = sbt("onesr", [1, 128])
        scale_sb = sbt("scale_sb", [128, 4])
        ident = sbt("ident", [128, 128], BF16)

        hid_sb = [sbt(f"hid{i}", [128, D]) for i in range(2)]
        hid_res = [sbt(f"hidr{i}", [128, D]) for i in range(2)]
        ln_sq = sbt("ln_sq", [128, D])
        stat = sbt("stat", [128, 16])
        xn_tm = [sbt(f"xntm{i}", [128, D], BF16) for i in range(2)]
        xnT = [sbt(f"xnT{i}", [128, 8 * T], BF16) for i in range(3)]
        x_raw = sbt("x_raw", [128, PC * (T + 3)])
        x_silu = [sbt(f"xsl{i}", [128, 8 * T]) for i in range(2)]
        xs_bf = sbt("xs_bf", [128, 8 * T], BF16)
        silz = [sbt(f"silz{i}", [128, 8 * T]) for i in range(2)]
        e_fl = sbt("e_fl", [128, 8 * T])
        g_fl = [sbt(f"g{i}", [128, 8 * T]) for i in range(2)]
        u_fl = [sbt(f"u{i}", [128, 8 * T]) for i in range(2)]
        xdbl = [sbt(f"xdbl{i}", [96, T]) for i in range(2)]
        xdblp = sbt("xdblp", [96, T])
        kkqk = [sbt(f"kkqk{i}", [32, T]) for i in range(2)]
        kk_sb = [sbt(f"kksb{i}", [16, T]) for i in range(2)]
        s1_sb = [sbt(f"s1sb{i}", [1, T]) for i in range(2)]
        dt_bf = [sbt(f"dtbf{i}", [64, T], BF16) for i in range(2)]
        reps = [sbt(f"reps{i}", [128, NPG * 3 * T]) for i in range(2)]
        s1rep = [sbt(f"s1rep{i}", [128, T]) for i in range(2)]
        gkk_b = [sbt(f"gkk{i}", [128, 8 * T]) for i in range(2)]
        b_fl = sbt("b_fl", [128, 8 * T])
        fix_sb = sbt("fix_sb", [128, 8])
        s_fl = sbt("s_fl", [128, 8 * T])
        y_fl = sbt("y_fl", [128, 8 * T])
        state = sbt("state", [128, 128])
        yg_bf = sbt("yg_bf", [128, 8 * T], BF16)
        ygall = [sbt(f"ygall{i}", [128, 16 * T], BF16) for i in range(1)]
        out_sb = [sbt("osb0", [128, D])]

        ps_ip = [psum("ps_ip0"), psum("ps_ip1")]
        ps_xp = psum("ps_xp")
        ps_dt = psum("ps_dt")
        ps_epi = [psum(f"ps_epi{i}") for i in range(4)]

        S = {}
        for nm in ["q_w", "s_set", "q_hid", "s_ln", "s_tr", "s_xnT", "s_ipmm",
                   "s_ipev", "s_conv", "s_xsilu", "s_xpmm", "s_xpev", "q_arst",
                   "s_ar", "q_arun", "s_kk", "s_skk", "s_s1", "q_reps", "s_dtmm",
                   "s_e", "s_den", "s_g", "s_u", "s_gb", "s_a", "s_scan",
                   "s_nblk", "s_gate", "q_agst", "s_ag", "q_agun", "s_epic",
                   "s_epimm", "s_epiev", "q_wst", "q_wst2", "q_wst3", "sv", "ss", "sg", "st", "q_stg"]:
            S[nm] = Sem(ctx.enter_context(nc.semaphore(nm)))
        for fam, nslot in [("q_w", 2), ("q_hid", 2), ("q_w1", 2), ("q_ow", 2),
                           ("q_hidres", 2), ("q_out", 2), ("q_grp", 2),
                           ("q_s1r", 2), ("q_arst", 2), ("q_arun", 2),
                           ("q_agst", 2), ("q_agun", 2)]:
            for i in range(nslot):
                nm = f"{fam}{i}"
                S[nm] = Sem(ctx.enter_context(nc.semaphore(nm)))

        wsc = [x_silu[0][:, :1024], x_silu[1][:, :1024]]
        wq1 = g_fl[0][:, :1024]
        wq2 = g_fl[1][:, :1024]
        blk = ctx.enter_context(nc.Block())
        P = Prog()

        def bc(tile_ap, nrep):
            return tile_ap.unsqueeze(1).broadcast_to(
                [tile_ap.shape[0], nrep, tile_ap.shape[1]])

        group_tok = {}
        qs_tok = {}
        cc_tok = {}
        gkk15_tok = {}
        grp_b_tok = {}
        b15_tok = {}
        stg_tok = {}
        s1_tok = {}
        hid_tok = {}
        w1_tok = {}
        ow_tok = {}
        hidr_tok = {}
        out_tok = {}
        arst_tok = {}
        arun_tok = {}
        agst_tok = {}
        agun_tok = {}

        def emit_group_load(G):
            t, g = G // N, G % N
            gp = G % 2
            qs_ = S[f"q_grp{gp}"]
            gw = [stg_tok[t]]
            if G >= 2:
                gw.append(grp_b_tok[G - 2])
                gw.append(qs_tok[G - 2])
            P.emit("scalar", (lambda t=t, g=g, gp=gp: lambda e: e.dma_start(
                out=reps[gp][:],
                in_=repsrc_d[t % 2][g * NPG * 3 * T:(g + 1) * NPG * 3 * T]
                    .unsqueeze(0).broadcast_to([128, NPG * 3 * T])))(),
                waits=gw, inc=qs_, inck=16)
            group_tok[G] = (qs_, qs_.count)

        P.emit("gpsimd", lambda e: e.memset(ident[:], 0.0), inc=S["sg"])
        _idm = S["sg"].count
        P.emit("gpsimd", lambda e: e.affine_select(
            out=ident[:], in_=ident[:], compare_op=ALU.not_equal, fill=1.0,
            base=0, pattern=[[-1, 128]], channel_multiplier=1),
            waits=[(S["sg"], _idm)], inc=S["sg"])

        P.emit("vector", lambda e: e.memset(y_fl[:], 0.0), inc=S["sv"])
        _zf = S["sv"].count
        fill_tok = {}
        for slot in range(2):
            qf = S[f"q_agst{slot}"]
            P.emit("sync", (lambda slot=slot: lambda e: e.dma_start(
                out=ccin_d[slot][0:YLEN].rearrange("(p t) -> p t", p=128),
                in_=y_fl[:, 0:YLEN // 128]))(),
                waits=[(S["sv"], _zf)], inc=qf, inck=16)
            P.emit("sync", (lambda slot=slot: lambda e: e.dma_start(
                out=ccin_d[slot][YLEN:].rearrange("(p t) -> p t", p=128),
                in_=y_fl[:, 0:XLEN // 128]))(),
                waits=[(qf, qf.count)], inc=qf, inck=16)
            fill_tok[slot] = (qf, qf.count)
        P.emit("vector", lambda e: e.memset(ones16[:], 1.0))
        P.emit("vector", lambda e: e.memset(onesc[:], 1.0))
        P.emit("vector", lambda e: e.memset(onesr[:], 1.0))
        P.emit("vector", lambda e: e.memset(scale_sb[:, 3:4], EPS))
        P.emit("vector", lambda e: e.memset(x_raw[:], 0.0))
        P.emit("vector", lambda e: e.memset(state[:], 0.0), inc=S["sv"])

        for fn in [
            lambda e: e.dma_start(out=cw_sb[:].rearrange("p (c f) -> p c f", c=PC),
                                  in_=cw_d[:].rearrange("c p f -> p c f"),
                                  allow_slow_non_contiguous=True),
            lambda e: e.dma_start(out=cb_sb[:].rearrange("p (c o) -> p c o", o=1),
                                  in_=cb_d[:].rearrange("c p o -> p c o"),
                                  allow_slow_non_contiguous=True),
            lambda e: e.dma_start(out=dv_sb[:].rearrange("p (c o) -> p c o", o=1),
                                  in_=dv_d[:].rearrange("c p o -> p c o"),
                                  allow_slow_non_contiguous=True),
            lambda e: e.dma_start(out=stat[:, 8:16].rearrange("p (c o) -> p c o", o=1),
                                  in_=dtb_d[:].rearrange("c p o -> p c o"),
                                  allow_slow_non_contiguous=True),
            lambda e: e.dma_start(out=ln_sq[:, :768], in_=xpt_d[:]),
            lambda e: e.dma_start(out=u_fl[0][:64, :1024], in_=dtht_d[:]),
        ]:
            P.emit("sync", fn, waits=[(S["q_w0"], S["q_w0"].count)],
                   inc=S["q_w0"], inck=16)
        q_small = S["q_w0"].count
        P.emit("vector", lambda e: e.tensor_scalar(
            out=ndtb_sb[:], in0=stat[:, 8:16], scalar1=-1.0, scalar2=None,
            op0=ALU.mult), waits=[(S["q_w0"], q_small)], inc=S["sv"])
        P.emit("vector", lambda e: e.tensor_copy(out=xpt_bf[:], in_=ln_sq[:, :768]),
               inc=S["sv"])
        P.emit("vector", lambda e: e.tensor_copy(out=dtht_bf[:], in_=u_fl[0][:64, :1024]),
               inc=S["sv"])
        set_small = S["sv"].count

        cast_done = {}
        absdone = {}
        w1st_done = {}
        ffqst_done = {}
        owst_done = {}
        for m in range(16):
            buf = wsc[m % 2]
            waits = [(S["ss"], cast_done[m - 2])] if m >= 2 else []
            waits.append((S["q_w0"], S["q_w0"].count))
            P.emit("sync", (lambda mm, bb: lambda e: e.dma_start(out=bb[:], in_=w1_d[mm]))(m, buf),
                   waits=waits, inc=S["q_w0"], inck=16)
            P.emit("scalar", (lambda mm, bb: lambda e: e.activation(
                out=xs_bf[:, :1024], in_=bb[:], func=AF.Identity))(m, buf),
                waits=[(S["q_w0"], S["q_w0"].count),
                       (S["q_wst"], w1st_done.get(m - 1, 0))], inc=S["ss"])
            cast_done[m] = S["ss"].count
            P.emit("scalar", (lambda mm: lambda e: e.dma_start(
                out=w1bf_d[mm], in_=xs_bf[:, :1024]))(m),
                waits=[(S["ss"], cast_done[m])], inc=S["q_wst"], inck=16)
            w1st_done[m] = S["q_wst"].count
        set_w1 = S["ss"].count

        for c in range(8):
            buf = wsc[c % 2]
            waits = ([(S["sv"], absdone[c - 2])] if c >= 2
                     else [(S["ss"], set_w1)])
            waits.append((S["q_w0"], S["q_w0"].count))
            P.emit("sync", (lambda cc, bb: lambda e: e.dma_start(out=bb[:], in_=fwt_d[cc]))(c, buf),
                   waits=waits, inc=S["q_w0"], inck=16)
            P.emit("vector", (lambda cc, bb: lambda e: e.tensor_reduce(
                out=stat[:, cc:cc + 1], in_=bb[:], axis=mybir.AxisListType.X,
                op=ALU.add, apply_absolute_value=True))(c, buf),
                waits=[(S["q_w0"], S["q_w0"].count)], inc=S["sv"])
            absdone[c] = S["sv"].count
        set_abs = S["sv"].count

        P.emit("vector", lambda e: e.tensor_reduce(
            out=stat[:, 8:9], in_=stat[:, 0:8], axis=mybir.AxisListType.X,
            op=ALU.add), waits=[(S["sv"], set_abs)], inc=S["sv"])
        P.emit("tensor", lambda e: e.matmul(out=ps_xp[0:1, 0:1], lhsT=stat[:, 8:9],
                                            rhs=onesc[:], start=True, stop=True),
               waits=[(S["sv"], S["sv"].count)], inc=S["st"])
        P.emit("vector", lambda e: e.tensor_scalar(
            out=stat[:1, 9:10], in0=ps_xp[0:1, 0:1], scalar1=1.0 / (1024.0 * 1024.0),
            scalar2=1e-5, op0=ALU.mult, op1=ALU.max),
            waits=[(S["st"], S["st"].count)], inc=S["sv"])
        P.emit("tensor", lambda e: e.matmul(out=ps_xp[:, 1:2], lhsT=onesr[:],
                                            rhs=stat[:1, 9:10], start=True, stop=True),
               waits=[(S["sv"], S["sv"].count)], inc=S["st"])
        P.emit("vector", lambda e: e.tensor_scalar(
            out=scale_sb[:, 0:1], in0=ps_xp[:, 1:2], scalar1=1.0, scalar2=None,
            op0=ALU.mult), waits=[(S["st"], S["st"].count)], inc=S["sv"])
        sc0 = S["sv"].count
        P.emit("vector", lambda e: e.tensor_scalar(
            out=scale_sb[:, 1:2], in0=scale_sb[:, 0:1], scalar1=0.5, scalar2=None,
            op0=ALU.mult), waits=[(S["sv"], sc0)], inc=S["sv"])
        P.emit("vector", lambda e: e.tensor_scalar(
            out=scale_sb[:, 2:3], in0=scale_sb[:, 0:1], scalar1=-0.5, scalar2=None,
            op0=ALU.mult), inc=S["sv"])
        P.emit("vector", lambda e: e.engine_nop(), inc=S["sv"])
        set_scale = S["sv"].count

        qdone = {}
        for c in range(8):
            buf = wsc[c % 2]
            waits = ([(S["sv"], qdone[c - 2])] if c >= 2
                     else [(S["sv"], set_scale)])
            waits.append((S["q_w0"], S["q_w0"].count))
            P.emit("sync", (lambda cc, bb: lambda e: e.dma_start(out=bb[:], in_=fwt_d[cc]))(c, buf),
                   waits=waits, inc=S["q_w0"], inck=16)
            qv = S["q_w0"].count
            P.emit("vector", (lambda cc, bb: lambda e: e.tensor_scalar(
                out=wq1[:], in0=bb[:], scalar1=scale_sb[:, 1:2], scalar2=None,
                op0=ALU.is_ge))(c, buf),
                waits=[(S["q_w0"], qv), (S["sv"], set_scale),
                       (S["sv"], qdone.get("yg", 0))], inc=S["sv"])
            P.emit("vector", (lambda cc, bb: lambda e: e.tensor_scalar(
                out=wq2[:], in0=bb[:], scalar1=scale_sb[:, 2:3], scalar2=None,
                op0=ALU.is_le))(c, buf),
                waits=[(S["sv"], qdone.get(c - 1, 0))], inc=S["sv"])
            t2v = S["sv"].count
            P.emit("vector", lambda e: e.tensor_tensor(
                out=wq1[:], in0=wq1[:], in1=wq2[:], op=ALU.subtract),
                waits=[(S["sv"], t2v)], inc=S["sv"])
            qdone[c] = S["sv"].count
            yw = [(S["q_wst2"], qdone.get("st", 0)), (S["sv"], qdone[c])]
            P.emit("vector", lambda e: e.tensor_scalar(
                out=yg_bf[:, :1024], in0=wq1[:],
                scalar1=scale_sb[:, 0:1], scalar2=None, op0=ALU.mult),
                waits=yw, inc=S["sv"])
            qdone["yg"] = S["sv"].count
            P.emit("scalar", (lambda cc: lambda e: e.dma_start(
                out=ffqd[cc], in_=yg_bf[:, :1024]))(c),
                waits=[(S["sv"], S["sv"].count)], inc=S["q_wst2"], inck=16)
            qdone["st"] = S["q_wst2"].count
            ffqst_done[c] = S["q_wst2"].count
        set_ffq = S["sv"].count

        owdone = {}
        for c in range(16):
            buf = wsc[c % 2]
            waits = [(S["ss"], owdone[c - 2])] if c >= 2 else [(S["sv"], set_ffq)]
            waits.append((S["q_w0"], S["q_w0"].count))
            P.emit("sync", (lambda cc, bb: lambda e: e.dma_start(out=bb[:], in_=owt_d[cc]))(c, buf),
                   waits=waits, inc=S["q_w0"], inck=16)
            qv = S["q_w0"].count
            P.emit("scalar", (lambda cc, bb: lambda e: e.activation(
                out=xs_bf[:, :1024], in_=bb[:], func=AF.Identity))(c, buf),
                waits=[(S["q_w0"], qv),
                       (S["q_wst3"], owst_done.get(c - 1, 0)),
                       (S["q_wst"], w1st_done[15] if c == 0 else 0)], inc=S["ss"])
            owdone[c] = S["ss"].count
            P.emit("scalar", (lambda cc: lambda e: e.dma_start(
                out=owbf_d[cc], in_=xs_bf[:, :1024]))(c),
                waits=[(S["ss"], owdone[c])], inc=S["q_wst3"], inck=16)
            owst_done[c] = S["q_wst3"].count

        SETUP_QW = S["q_w0"].count
        SETUP_QWST = S["q_wst"].count
        SETUP_QWST2 = S["q_wst2"].count
        SETUP_QWST3 = S["q_wst3"].count
        SETUP_SV = S["sv"].count
        r2_done = [0]


        def prep_ln(tp):
            for s in range(SUB):
                slot = s
                P.emit("scalar", (lambda slot=slot: lambda e: e.activation(
                    out=ln_sq[:], in_=hid_sb[slot][:], func=AF.Square))(),
                    waits=[hid_tok[(tp, s)], (S["sv"], r2_done[0])],
                    inc=S["ss"])
                sqv = S["ss"].count
                P.emit("vector", (lambda slot=slot: lambda e: e.tensor_reduce(
                    out=stat[:, 0:1], in_=hid_sb[slot][:],
                    axis=mybir.AxisListType.X, op=ALU.add))(),
                    waits=[hid_tok[(tp, s)]], inc=S["sv"])
                r1v = S["sv"].count
                P.emit("vector", lambda e: e.tensor_reduce(
                    out=stat[:, 1:2], in_=ln_sq[:], axis=mybir.AxisListType.X,
                    op=ALU.add), waits=[(S["ss"], sqv)], inc=S["sv"])
                r2_done[0] = S["sv"].count
                P.emit("vector", lambda e: e.tensor_scalar(
                    out=stat[:, 2:3], in0=stat[:, 0:1], scalar1=1.0 / D,
                    scalar2=None, op0=ALU.mult), waits=[(S["sv"], r1v)],
                    inc=S["sv"])
                v1 = S["sv"].count
                P.emit("vector", lambda e: e.tensor_scalar(
                    out=stat[:, 3:4], in0=stat[:, 1:2], scalar1=1.0 / D,
                    scalar2=None, op0=ALU.mult), waits=[(S["sv"], r2_done[0])],
                    inc=S["sv"])
                v2 = S["sv"].count
                P.emit("vector", lambda e: e.tensor_tensor(
                    out=stat[:, 4:5], in0=stat[:, 2:3], in1=stat[:, 2:3],
                    op=ALU.mult), waits=[(S["sv"], v1)], inc=S["sv"])
                v3 = S["sv"].count
                P.emit("vector", lambda e: e.tensor_tensor(
                    out=stat[:, 5:6], in0=stat[:, 3:4], in1=stat[:, 4:5],
                    op=ALU.subtract), waits=[(S["sv"], v3)], inc=S["sv"])
                varv = S["sv"].count
                P.emit("scalar", lambda e: e.activation(
                    out=stat[:, 6:7], in_=stat[:, 5:6], func=AF.Ln, bias=scale_sb[:, 3:4]),
                    waits=[(S["sv"], varv)], inc=S["ss"])
                lnv = S["ss"].count
                P.emit("scalar", lambda e: e.activation(
                    out=stat[:, 7:8], in_=stat[:, 6:7], func=AF.Exp, scale=-0.5),
                    waits=[(S["ss"], lnv)], inc=S["ss"])
                rsv = S["ss"].count
                P.emit("vector", lambda e: e.scalar_tensor_tensor(
                    out=stat[:, 8:9], in0=stat[:, 2:3], scalar=-1.0,
                    in1=stat[:, 7:8], op0=ALU.mult, op1=ALU.mult),
                    waits=[(S["ss"], rsv)], inc=S["sv"])
                ngv = S["sv"].count
                P.emit("vector", (lambda slot=slot, s=s: lambda e: e.tensor_scalar(
                    out=xn_tm[s][:], in0=hid_sb[slot][:],
                    scalar1=stat[:, 7:8], scalar2=stat[:, 8:9],
                    op0=ALU.mult, op1=ALU.add))(),
                    waits=[(S["sv"], ngv)], inc=S["s_ln"])

        def prep_transp_ip(tp):
            for t in range(16):
                s, kc = t // 8, t % 8
                bank = ps_xp if t % 2 == 0 else ps_dt
                tw = [(S["s_ln"], 2 * tp + s + 1)]
                if tp == 0 and t == 0:
                    tw.append((S["sg"], 2))
                if t >= 2:
                    tw.append((S["s_xnT"], 16 * tp + t - 1))
                if t < 2:
                    tw.append((S["s_s1"], tp))
                    tw.append((S["s_e"], 8 * tp))
                P.emit("tensor", (lambda s=s, kc=kc, bank=bank: lambda e:
                                  e.transpose(out=bank[:, 0:64].bitcast(BF16),
                                              in_=xn_tm[s][:, kc * 128:(kc + 1) * 128],
                                              identity=ident[:]))(),
                       waits=tw, inc=S["s_tr"])
                P.emit("scalar", (lambda tp=tp, s=s, kc=kc, bank=bank, t=t: lambda e:
                                  e.activation(
                    out=xnT[tp % 3][:, kc * T + s * 128:kc * T + (s + 1) * 128],
                    in_=bank[:, 0:64].bitcast(BF16), func=AF.Identity))(),
                    waits=[(S["s_tr"], 16 * tp + t + 1)], inc=S["s_xnT"])
            for m in range(16):
                bank = ps_ip[m % 2]
                for k in range(8):
                    mw = []
                    if k == 0:
                        mw.append((S["s_xnT"], 16 * (tp + 1)))
                        mw.append((S["s_ipev"], 16 * tp + m - 1))
                        mw.append(w1_tok[(tp, m)])
                    P.emit("tensor", (lambda tp=tp, m=m, k=k, bank=bank: lambda e:
                                      e.matmul(
                        out=bank[:, 0:T],
                        lhsT=w1_buf[m % 2][:, k * 128:(k + 1) * 128],
                        rhs=xnT[tp % 3][:, k * T:(k + 1) * T],
                        start=(k == 0), stop=(k == 7)))(),
                        waits=mw, inc=(S["s_ipmm"] if k == 7 else None))
                if m < 8:
                    ew = [(S["s_ipmm"], 16 * tp + m + 1), (S["s_conv"], tp)]
                    P.emit("scalar", (lambda m=m, bank=bank: lambda e: e.activation(
                        out=x_raw[:, m * (T + 3) + 3:(m + 1) * (T + 3)],
                        in_=bank[:, 0:T], func=AF.Identity))(),
                        waits=ew, inc=S["s_ipev"])
                else:
                    ew = [(S["s_ipmm"], 16 * tp + m + 1), (S["s_gate"], tp - 1)]
                    P.emit("scalar", (lambda tp=tp, m=m, bank=bank: lambda e:
                                      e.activation(
                        out=silz[tp % 2][:, (m - 8) * T:(m - 7) * T],
                        in_=bank[:, 0:T], func=AF.Silu))(),
                        waits=ew, inc=S["s_ipev"])

        def prep_conv(tp):
            cw0 = [(S["s_ipev"], 16 * tp + 8), (S["s_gate"], tp - 1)]
            if tp - 2 in b15_tok:
                cw0.append(b15_tok[tp - 2])
            if tp == 0:
                cw0.append((S["q_w0"], SETUP_QW))
                cw0.append((S["q_wst"], SETUP_QWST))
                cw0.append((S["q_wst2"], SETUP_QWST2))
                cw0.append((S["q_wst3"], SETUP_QWST3))
            xr3 = x_raw[:].rearrange("p (c t) -> p c t", t=T + 3)
            cw3 = cw_sb[:].rearrange("p (c f) -> p c f", f=4)

            def cwbc(tap):
                return cw3[:, :, tap].unsqueeze(2).broadcast_to([128, 8, T])

            xs3 = x_silu[tp % 2][:].rearrange("p (c t) -> p c t", t=T)
            u3 = u_fl[tp % 2][:].rearrange("p (c t) -> p c t", t=T)
            P.emit("vector", (lambda tp=tp, xr3=xr3: lambda e: e.tensor_tensor(
                out=xs3, in0=xr3[:, :, 0:T], in1=cwbc(0), op=ALU.mult))(),
                waits=cw0, inc=S["sv"])
            for tap in (1, 2, 3):
                tv = S["sv"].count
                P.emit("vector", (lambda tap=tap, xr3=xr3, u3=u3: lambda e:
                                  e.tensor_tensor(
                    out=u3, in0=xr3[:, :, tap:tap + T], in1=cwbc(tap),
                    op=ALU.mult))(), waits=[(S["sv"], tv)], inc=S["sv"])
                tv = S["sv"].count
                P.emit("vector", (lambda tp=tp: lambda e: e.tensor_tensor(
                    out=x_silu[tp % 2][:], in0=x_silu[tp % 2][:],
                    in1=u_fl[tp % 2][:], op=ALU.add))(),
                    waits=[(S["sv"], tv)], inc=S["sv"])
            tv = S["sv"].count
            P.emit("vector", (lambda xr3=xr3: lambda e: e.tensor_copy(
                out=xr3[:, :, 0:3], in_=xr3[:, :, T:T + 3]))(),
                waits=[(S["sv"], tv)], inc=S["sv"])
            P.emit("vector", lambda e: e.engine_nop(), inc=S["s_conv"])
            for pc in range(PC):
                P.emit("scalar", (lambda tp=tp, pc=pc: lambda e: e.activation(
                    out=x_silu[tp % 2][:, pc * T:(pc + 1) * T],
                    in_=x_silu[tp % 2][:, pc * T:(pc + 1) * T], func=AF.Silu,
                    bias=cb_sb[:, pc:pc + 1]))(),
                    waits=([(S["s_conv"], tp + 1)] if pc == 0 else []),
                    inc=(S["ss"] if pc == PC - 1 else None))
            xw = [(S["s_xpmm"], tp), (S["ss"], S["ss"].count)]
            if tp == 0:
                xw.append((S["q_wst3"], SETUP_QWST3))
            P.emit("scalar", (lambda tp=tp: lambda e: e.activation(
                out=xs_bf[:], in_=x_silu[tp % 2][:], func=AF.Identity))(),
                waits=xw, inc=S["s_xsilu"])

        def prep_xp_ar(tp):
            for k in range(8):
                xw = ([(S["s_xsilu"], tp + 1), (S["s_xnT"], 16 * (tp + 1)),
                       (S["s_s1"], tp)] if k == 0 else [])
                P.emit("tensor", (lambda k=k: lambda e: e.matmul(
                    out=ps_xp[0:96, 0:T],
                    lhsT=xpt_bf[:, k * 96:(k + 1) * 96],
                    rhs=xs_bf[:, k * T:(k + 1) * T],
                    start=(k == 0), stop=(k == 7)))(),
                    waits=xw, inc=(S["s_xpmm"] if k == 7 else None))
            P.emit("scalar", lambda e: e.activation(
                out=xdblp[:], in_=ps_xp[0:96, 0:T], func=AF.Identity),
                waits=[(S["s_xpmm"], tp + 1)], inc=S["s_xpev"])
            xpw_ = [(S["s_xpev"], tp + 1)]
            if tp - 3 in cc_tok:
                xpw_.append(cc_tok[tp - 3])
            if tp < 2:
                xpw_.append(fill_tok[tp % 2])
            P.emit("scalar", (lambda tp=tp: lambda e: e.dma_start(
                out=ccin_d[tp % 2][YLEN:].rearrange("(p t) -> p t", p=96),
                in_=xdblp[:]))(),
                waits=xpw_, inc=S[f"q_arst{tp % 2}"], inck=16)
            arst_tok[tp] = (S[f"q_arst{tp % 2}"], S[f"q_arst{tp % 2}"].count)

        def collective_cc(j):
            tp, te = j + 1, j - 1
            slot = (j + 1) % 2
            cw_ = []
            if 0 <= tp < NT:
                cw_.append(arst_tok[tp])
            if 0 <= te < NT:
                cw_.append(agst_tok[te])
            if te - 2 >= 0:
                cw_.append(agun_tok[te - 2])
            if tp - 2 >= 0 and (tp - 2) in arun_tok:
                cw_.append(arun_tok[tp - 2])
            if j <= 0:
                cw_.append(fill_tok[slot])
            P.emit("gpsimd", (lambda slot=slot: lambda e: e.collective_compute(
                "AllGather", ALU.bypass, replica_groups=GROUPS,
                ins=[ccin_d[slot][:]], outs=[ccout_d[slot][:]]))(),
                waits=cw_, inc=S["s_ag"])
            cc_tok[j] = (S["s_ag"], S["s_ag"].count)

        def prep_post_ar(tp):
            qa = S[f"q_arun{tp % 2}"]
            P.emit("scalar", (lambda tp=tp: lambda e: e.dma_start(
                out=xdbl[tp % 2][:],
                in_=ccout_d[tp % 2][0, YLEN:]
                    .rearrange("(p t) -> p t", p=96)))(),
                waits=[cc_tok[tp - 1]], inc=qa, inck=16)
            t0_ = (qa, qa.count)
            P.emit("scalar", (lambda tp=tp: lambda e: e.dma_start(
                out=xdblp[:],
                in_=ccout_d[tp % 2][1, YLEN:]
                    .rearrange("(p t) -> p t", p=96)))(),
                waits=[t0_, arst_tok[tp]], inc=qa, inck=16)
            arun_tok[tp] = (qa, qa.count)
            P.emit("vector", (lambda tp=tp: lambda e: e.tensor_tensor(
                out=xdbl[tp % 2][:], in0=xdbl[tp % 2][:], in1=xdblp[:],
                op=ALU.add))(), waits=[arun_tok[tp]], inc=S["sv"])
            xdbl_tok = (S["sv"], S["sv"].count)
            P.emit("scalar", (lambda tp=tp: lambda e: e.activation(
                out=kk_sb[tp % 2][:], in_=xdbl[tp % 2][64:80, :],
                func=AF.Square))(), waits=[xdbl_tok])
            P.emit("scalar", (lambda tp=tp: lambda e: e.activation(
                out=kkqk[tp % 2][0:32, :], in_=xdbl[tp % 2][64:96, :],
                func=AF.Identity))())
            P.emit("scalar", (lambda tp=tp: lambda e: e.activation(
                out=dt_bf[tp % 2][:], in_=xdbl[tp % 2][0:64, :],
                func=AF.Identity))(), inc=S["s_kk"])
            P.emit("tensor", (lambda tp=tp: lambda e: e.matmul(
                out=ps_xp[0:1, 256:256 + T], lhsT=ones16[:],
                rhs=kk_sb[tp % 2][:], start=True, stop=True))(),
                waits=[(S["s_kk"], tp + 1), (S["s_xpev"], tp + 1)],
                inc=S["s_skk"])
            P.emit("scalar", (lambda tp=tp: lambda e: e.activation(
                out=s1_sb[tp % 2][:], in_=ps_xp[0:1, 256:256 + T],
                func=AF.Identity, bias=1.0))(),
                waits=[(S["s_skk"], tp + 1)], inc=S["s_s1"])
            for m in range(8):
                dw = [(S["s_kk"], tp + 1), (S["s_e"], 8 * tp + m)]
                P.emit("tensor", (lambda tp=tp, m=m: lambda e: e.matmul(
                    out=ps_dt[:, 0:T],
                    lhsT=dtht_bf[:, m * 128:(m + 1) * 128],
                    rhs=dt_bf[tp % 2][:], start=True, stop=True))(),
                    waits=dw, inc=S["s_dtmm"])
                P.emit("scalar", (lambda tp=tp, m=m: lambda e: e.activation(
                    out=e_fl[:, m * T:(m + 1) * T], in_=ps_dt[:, 0:T],
                    func=AF.Exp, bias=ndtb_sb[:, m:m + 1], scale=-1.0))(),
                    waits=[(S["s_dtmm"], 8 * tp + m + 1)], inc=S["s_e"])

        def prep_reps(tp):
            srcs = [(kk_sb[tp % 2][0:16, :], 0), (kkqk[tp % 2][0:16, :], 1),
                    (kkqk[tp % 2][16:32, :], 2)]
            for i, (sap, off) in enumerate(srcs):
                sw_ = [(S["q_stg"], S["q_stg"].count)]
                if i == 0:
                    sw_.append((S["s_kk"], tp + 1))
                P.emit("scalar", (lambda sap=sap, tp=tp, off=off: lambda e:
                                  e.dma_start(
                    out=repsrc_d[tp % 2][:48 * T].rearrange(
                        "(n a t) -> n a t", n=16, a=3)[:, off, :],
                    in_=sap))(),
                    waits=sw_, inc=S["q_stg"], inck=16)
            P.emit("scalar", (lambda tp=tp: lambda e: e.dma_start(
                out=repsrc_d[tp % 2][48 * T:49 * T], in_=s1_sb[tp % 2][:]))(),
                waits=[(S["s_s1"], tp + 1), (S["q_stg"], S["q_stg"].count)],
                inc=S["q_stg"], inck=16)
            stg_tok[tp] = (S["q_stg"], S["q_stg"].count)
            qs1 = S[f"q_s1r{tp % 2}"]
            P.emit("scalar", (lambda tp=tp: lambda e: e.dma_start(
                out=s1rep[tp % 2][:],
                in_=repsrc_d[tp % 2][48 * T:49 * T].unsqueeze(0)
                    .broadcast_to([128, T])))(),
                waits=[(S["s_den"], 2 * (tp - 1)), stg_tok[tp]],
                inc=qs1, inck=16)
            s1_tok[tp] = (qs1, qs1.count)
            if tp == 0:
                emit_group_load(0)
                emit_group_load(1)

        def prep_dtv(tp):
            for h in range(2):
                dw = []
                if h == 0:
                    dw = [(S["s_e"], 8 * (tp + 1)), s1_tok[tp]]
                    if tp - 2 in gkk15_tok:
                        dw.append(gkk15_tok[tp - 2])
                P.emit("vector", (lambda tp=tp, h=h: lambda e: e.tensor_tensor(
                    out=g_fl[tp % 2][:, h * 4 * T:(h + 1) * 4 * T].rearrange(
                        "p (c t) -> p c t", t=T),
                    in0=e_fl[:, h * 4 * T:(h + 1) * 4 * T].rearrange(
                        "p (c t) -> p c t", t=T),
                    in1=bc(s1rep[tp % 2][:], 4), op=ALU.add))(),
                    waits=dw, inc=S["s_den"])
            dnv = S["s_den"].count
            P.emit("scalar", (lambda tp=tp: lambda e: e.activation(
                out=e_fl[:], in_=g_fl[tp % 2][:], func=AF.Ln))(),
                waits=[(S["s_den"], dnv)], inc=S["s_g"])
            lnv = S["s_g"].count
            P.emit("scalar", (lambda tp=tp: lambda e: e.activation(
                out=g_fl[tp % 2][:], in_=e_fl[:], func=AF.Exp, scale=-1.0))(),
                waits=[(S["s_g"], lnv)], inc=S["s_g"])

        def prep_u(tp):
            P.emit("vector", (lambda tp=tp: lambda e: e.tensor_tensor(
                out=u_fl[tp % 2][:], in0=x_silu[tp % 2][:], in1=g_fl[tp % 2][:],
                op=ALU.mult))(),
                waits=[(S["s_g"], 2 * (tp + 1))] +
                      ([b15_tok[tp - 2]] if tp - 2 in b15_tok else []),
                inc=S["s_u"])

        def scan_nblock(j, n):
            jp = j % 2
            G = N * j + n
            np_ = n % 2
            gp = G % 2
            kkr = reps[gp][:, 0 * T:1 * T]
            kr = reps[gp][:, 1 * T:2 * T]
            qr = reps[gp][:, 2 * T:3 * T]
            gw = [group_tok[G]]
            if n == 0:
                gw += [(S["s_g"], 2 * (j + 1)), (S["s_u"], j + 1)]
            if G >= 2:
                gw.append((S["s_scan"], G - 1))
            P.emit("gpsimd", (lambda jp=jp, np_=np_, kkr=kkr: lambda e:
                              e.tensor_tensor(
                out=gkk_b[np_][:].rearrange("p (c t) -> p c t", t=T),
                in0=g_fl[jp][:].rearrange("p (c t) -> p c t", t=T),
                in1=bc(kkr, 8), op=ALU.mult))(), waits=gw, inc=S["s_gb"])
            gkk_tok = (S["s_gb"], S["s_gb"].count)
            if n == N - 1:
                gkk15_tok[j] = gkk_tok
            bw = []
            if G >= 1:
                bw.append((S["s_scan"], G))
            P.emit("gpsimd", (lambda jp=jp, kr=kr: lambda e: e.tensor_tensor(
                out=b_fl[:].rearrange("p (c t) -> p c t", t=T),
                in0=u_fl[jp][:].rearrange("p (c t) -> p c t", t=T),
                in1=bc(kr, 8), op=ALU.mult))(), waits=bw, inc=S["s_gb"])
            b_tok = (S["s_gb"], S["s_gb"].count)
            grp_b_tok[G] = b_tok
            if n == N - 1:
                b15_tok[j] = b_tok
            P.emit("scalar", (lambda np_=np_: lambda e: e.activation(
                out=gkk_b[np_][:], in_=gkk_b[np_][:], func=AF.Identity,
                bias=1.0, scale=-1.0))(),
                waits=[gkk_tok], inc=S["s_a"])
            a_tok = (S["s_a"], S["s_a"].count)
            a3 = gkk_b[np_][:].rearrange("p (c t) -> p c t", t=T)
            b3 = b_fl[:].rearrange("p (c t) -> p c t", t=T)
            abound = a3[:, 1:8, 0]
            if j >= 1:
                stv = state[:].rearrange("p (c nn) -> p c nn", nn=16)[:, 1:8, n]
                P.emit("gpsimd", (lambda abound=abound, stv=stv: lambda e:
                                  e.tensor_tensor(
                    out=fix_sb[:, 0:7], in0=abound, in1=stv, op=ALU.mult))(),
                    waits=[a_tok, (S["s_nblk"], 16 * (j - 1) + n + 1)],
                    inc=S["s_gb"])
                tmp_tok = (S["s_gb"], S["s_gb"].count)
                P.emit("gpsimd", (lambda b3=b3: lambda e: e.tensor_tensor(
                    out=b3[:, 1:8, 0], in0=b3[:, 1:8, 0], in1=fix_sb[:, 0:7],
                    op=ALU.add))(), waits=[b_tok, tmp_tok], inc=S["s_gb"])
                bfix_tok = (S["s_gb"], S["s_gb"].count)
                zw = [tmp_tok]
            else:
                bfix_tok = b_tok
                zw = [a_tok]
            P.emit("scalar", (lambda abound=abound: lambda e: e.activation(
                out=abound, in_=abound, func=AF.Identity, scale=0.0))(),
                waits=zw, inc=S["s_a"])
            zero_tok = (S["s_a"], S["s_a"].count)
            sw = [zero_tok, bfix_tok, group_tok[G],
                  (S["sv"], S["sv"].count)]
            if n == 0 and j >= 1:
                sw.append((S["s_gate"], j))
            init = 0.0 if j == 0 else state[:, n:n + 1]
            P.emit("vector", (lambda np_=np_, init=init: lambda e:
                              e.tensor_tensor_scan(
                out=s_fl[:], data0=gkk_b[np_][:], data1=b_fl[:],
                initial=init, op0=ALU.mult, op1=ALU.add))(),
                waits=sw, inc=S["s_scan"])
            P.emit("vector", (lambda n=n: lambda e: e.tensor_copy(
                out=state[:].rearrange("p (c nn) -> p c nn", nn=16)[:, :, n],
                in_=s_fl[:].rearrange("p (c t) -> p c t", t=T)[:, :, T - 1]))(),
                waits=[(S["s_scan"], G + 1)], inc=S["s_nblk"])
            P.emit("vector", (lambda qr=qr: lambda e: e.tensor_tensor(
                out=s_fl[:].rearrange("p (c t) -> p c t", t=T),
                in0=s_fl[:].rearrange("p (c t) -> p c t", t=T),
                in1=bc(qr, 8), op=ALU.mult))(),
                waits=[(S["s_nblk"], S["s_nblk"].count)], inc=S["sv"])
            qs_tok[G] = (S["sv"], S["sv"].count)
            if n == 0:
                yw_ = [qs_tok[G]]
                if j == 0:
                    yw_ += [fill_tok[0], fill_tok[1]]
                P.emit("vector", (lambda yw_=yw_: lambda e: e.tensor_copy(
                    out=y_fl[:], in_=s_fl[:]))(),
                    waits=yw_, inc=S["sv"])
            else:
                P.emit("vector", lambda e: e.tensor_tensor(
                    out=y_fl[:], in0=y_fl[:], in1=s_fl[:], op=ALU.add),
                    waits=[qs_tok[G]], inc=S["sv"])
            Gn = G + 2
            if Gn < N * NT:
                emit_group_load(Gn)

        def epi_mms(te):
            for c in range(16):
                lw = [ow_tok[(te, c)], agun_tok[te]]
                if c == 0:
                    lw.append((S["s_epiev"], 2 * te))
                for sub in range(SUB):
                    for hf in range(2):
                        P.emit("tensor", (lambda te=te, c=c, sub=sub, hf=hf:
                                          lambda e: e.matmul(
                            out=ps_epi[sub * 2 + hf][:, 0:512],
                            lhsT=ygall[0][:, c * T + sub * 128:
                                               c * T + (sub + 1) * 128],
                            rhs=ow_buf[c % 2][:, hf * 512:(hf + 1) * 512],
                            start=(c == 0), stop=False))(),
                            waits=(lw if (sub == 0 and hf == 0) else []),
                            inc=(S["s_epic"] if (sub == 1 and hf == 1)
                                 else None))
            for cc in range(8):
                fw = [ow_tok[(te, 16 + cc)]]
                for sub in range(SUB):
                    for hf in range(2):
                        P.emit("tensor", (lambda te=te, cc=cc, sub=sub, hf=hf:
                                          lambda e: e.matmul(
                            out=ps_epi[sub * 2 + hf][:, 0:512],
                            lhsT=xnT[te % 3][:, cc * T + sub * 128:
                                             cc * T + (sub + 1) * 128],
                            rhs=ow_buf[(16 + cc) % 2][:, hf * 512:(hf + 1) * 512],
                            start=False, stop=(cc == 7)))(),
                            waits=(fw if (sub == 0 and hf == 0) else []),
                            inc=(S["s_epic"] if (sub == 1 and hf == 1)
                                 else None))

        def epi_evacs(te):
            for sub in range(SUB):
                for hf in range(2):
                    ew = [(S["s_epic"], 24 * (te + 1))]
                    if hf == 0:
                        ew.append(hidr_tok[(te, sub)])
                        if sub == 1:
                            ew.append(out_tok[(te, 0)])
                        elif te >= 1:
                            ew.append(out_tok[(te - 1, 1)])
                    P.emit("vector", (lambda sub=sub, hf=hf:
                                      lambda e: e.tensor_tensor(
                        out=out_sb[0][:, hf * 512:(hf + 1) * 512],
                        in0=ps_epi[sub * 2 + hf][:, 0:512],
                        in1=hid_res[sub][:, hf * 512:(hf + 1) * 512],
                        op=ALU.add))(), waits=ew,
                        inc=(S["s_epiev"] if hf == 1 else None))
                qo = S["q_out0"]
                P.emit("sync", (lambda te=te, sub=sub: lambda e: e.dma_start(
                    out=out_d[(te * SUB + sub) * 128:
                              (te * SUB + sub + 1) * 128, :],
                    in_=out_sb[0][:]))(),
                    waits=[(S["s_epiev"], 2 * te + sub + 1),
                           (qo, qo.count)],
                    inc=qo, inck=16)
                out_tok[(te, sub)] = (qo, qo.count)

        def gate(j):
            for pc in range(PC):
                gw = ([(S["s_ipev"], 16 * (j + 1)),
                       (S["sv"], S["sv"].count)] if pc == 0 else [])
                P.emit("vector", (lambda j=j, pc=pc: lambda e:
                                  e.scalar_tensor_tensor(
                    out=s_fl[:, pc * T:(pc + 1) * T],
                    in0=x_silu[j % 2][:, pc * T:(pc + 1) * T],
                    scalar=dv_sb[:, pc:pc + 1],
                    in1=y_fl[:, pc * T:(pc + 1) * T],
                    op0=ALU.mult, op1=ALU.add))(), waits=gw,
                    inc=(S["sv"] if pc == PC - 1 else None))
            P.emit("vector", (lambda j=j: lambda e: e.tensor_tensor(
                out=yg_bf[:], in0=s_fl[:], in1=silz[j % 2][:], op=ALU.mult))(),
                waits=([agst_tok[j - 1]] if j >= 1 else []) +
                      [(S["sv"], S["sv"].count)],
                inc=S["s_gate"])
            gsw = [(S["s_gate"], j + 1)]
            if j - 1 in cc_tok:
                gsw.append(cc_tok[j - 1])
            P.emit("scalar", (lambda j=j: lambda e: e.dma_start(
                out=ccin_d[j % 2][0:YLEN].bitcast(BF16).rearrange(
                    "(c p t) -> p c t", p=128, t=T),
                in_=yg_bf[:].rearrange("p (c t) -> p c t", c=PC)))(),
                waits=gsw, inc=S[f"q_agst{j % 2}"], inck=16)
            agst_tok[j] = (S[f"q_agst{j % 2}"], S[f"q_agst{j % 2}"].count)

        def emit_agun(te):
            uw = [cc_tok[te + 1], (S["s_epic"], 24 * te)]
            qag = S[f"q_agun{te % 2}"]
            for r in range(2):
                P.emit("scalar", (lambda te=te, r=r: lambda e: e.dma_start(
                    out=ygall[0][:, r * 8 * T:(r + 1) * 8 * T].rearrange(
                        "p (c t) -> p c t", t=T),
                    in_=ccout_d[te % 2][r, 0:YLEN].bitcast(BF16).rearrange(
                        "(c p t) -> p c t", p=128, t=T)))(),
                    waits=(uw if r == 0 else [(qag, qag.count)]),
                    inc=qag, inck=16)
            agun_tok[te] = (qag, qag.count)

        for j in range(-1, NT + 1):
            tp, te = j + 1, j - 1
            do_scan = 0 <= j < NT
            do_prep = 0 <= tp < NT
            do_epi = 0 <= te < NT

            if do_prep:
                for s in range(SUB):
                    hw = []
                    if tp >= 1:
                        hw.append((S["s_ln"], 2 * (tp - 1) + s + 1))
                    qh = S[f"q_hid{s}"]
                    P.emit("sync", (lambda tp=tp, s=s: lambda e:
                                    e.dma_start(
                        out=hid_sb[s][:],
                        in_=hid_d[(tp * SUB + s) * 128:(tp * SUB + s + 1) * 128, :]))(),
                        waits=hw, inc=qh, inck=16)
                    hid_tok[(tp, s)] = (qh, qh.count)
                for m in range(16):
                    ww = [(S["q_wst"], w1st_done[m])]
                    if 16 * tp + m >= 2:
                        ww.append((S["s_ipmm"], 16 * tp + m - 1))
                    qw1 = S[f"q_w1{m % 2}"]
                    P.emit("sync", (lambda m=m: lambda e: e.dma_start(
                        out=w1_buf[m % 2][:], in_=w1bf_d[m]))(),
                        waits=ww, inc=qw1, inck=16)
                    w1_tok[(tp, m)] = (qw1, qw1.count)
            if do_epi:
                for c in range(24):
                    ow = [(S["q_wst3"], owst_done[c]) if c < 16 else (S["q_wst2"], ffqst_done[c - 16])]
                    if 24 * te + c >= 2:
                        ow.append((S["s_epic"], 24 * te + c - 1))
                    src_d = owbf_d[c] if c < 16 else ffqd[c - 16]
                    qow = S[f"q_ow{c % 2}"]
                    P.emit("sync", (lambda c=c, src_d=src_d: lambda e: e.dma_start(
                        out=ow_buf[c % 2][:], in_=src_d))(),
                        waits=ow, inc=qow, inck=16)
                    ow_tok[(te, c)] = (qow, qow.count)
                for s in range(SUB):
                    hw = []
                    if te >= 1:
                        hw.append((S["s_epiev"], 2 * (te - 1) + s + 1))
                    qhr = S[f"q_hidres{s}"]
                    P.emit("sync", (lambda te=te, s=s: lambda e: e.dma_start(
                        out=hid_res[s][:],
                        in_=hid_d[(te * SUB + s) * 128:(te * SUB + s + 1) * 128, :]))(),
                        waits=hw, inc=qhr, inck=16)
                    hidr_tok[(te, s)] = (qhr, qhr.count)

            inserts = {
                1: (lambda: prep_ln(tp)) if do_prep else None,
                2: (lambda: prep_transp_ip(tp)) if do_prep else None,
                5: (lambda: prep_conv(tp)) if do_prep else None,
                6: (lambda: prep_xp_ar(tp)) if do_prep else None,
                7: (lambda: (collective_cc(j) if (do_prep or do_epi) else None,
                             emit_agun(te) if do_epi else None,
                             epi_mms(te) if do_epi else None)),
                8: (lambda: epi_evacs(te)) if do_epi else None,
                10: (lambda: prep_post_ar(tp)) if do_prep else None,
                11: (lambda: prep_reps(tp)) if do_prep else None,
                13: (lambda: prep_dtv(tp)) if do_prep else None,
                14: (lambda: prep_u(tp)) if do_prep else None,
            }
            if do_scan:
                for n in range(N):
                    scan_nblock(j, n)
                    if inserts.get(n + 1) is not None:
                        inserts[n + 1]()
                        inserts[n + 1] = None
                gate(j)
            else:
                for key in sorted(k for k in inserts if inserts[k] is not None):
                    inserts[key]()


        P.emit("sync", lambda e: e.nop(),
               waits=[(S["q_out0"], S["q_out0"].count),
                      (S["q_out1"], S["q_out1"].count)])

        dbg_d = {}
        if debug:
            taps = [("hid0", hid_sb[0]), ("hid1", hid_sb[1]),
                    ("xntm0", xn_tm[0]), ("xntm1", xn_tm[1]),
                    ("statx", stat), ("xnt0", xnT[0]), ("xraw0", x_raw), ("xsilu0", x_silu[0]),
                    ("silz0", silz[0]), ("e0", e_fl), ("g0", g_fl[0]),
                    ("u0", u_fl[0]), ("y0", y_fl), ("state0", state),
                    ("s1rep0", s1rep[0]), ("xdbl0", xdbl[0]),
                    ("yg0", yg_bf), ("ygall0", ygall[0])]
            for nm, buf in taps:
                d = nc.dram_tensor("dbg_" + nm, list(buf.shape), buf.dtype,
                                   kind="ExternalOutput").ap()
                dbg_d[nm] = d
                P.emit("sync", (lambda d=d, buf=buf: lambda e: e.dma_start(
                    out=d, in_=buf[:]))(),
                    waits=[(S["q_out0"], S["q_out0"].count)],
                    inc=S["q_out0"], inck=16)
            P.emit("sync", lambda e: e.nop(),
                   waits=[(S["q_out0"], S["q_out0"].count)])

        @blk.sync
        def _(e):
            P.run("sync", e)

        @blk.scalar
        def _(e):
            P.run("scalar", e)

        @blk.vector
        def _(e):
            P.run("vector", e)

        @blk.tensor
        def _(e):
            P.run("tensor", e)

        @blk.gpsimd
        def _(e):
            P.run("gpsimd", e)

    return nc


def host_inputs(inputs, NT=32):
    L = NT * T
    f = np.float32
    hid = inputs["L0_attn_hidden_states"][:, :L]
    ipw = inputs["L0_attn_in_proj_w"]
    cw = inputs["L0_attn_conv_w"]
    cb = inputs["L0_attn_conv_b"]
    xpw = inputs["L0_attn_x_proj_w"]
    dthw = inputs["L0_attn_dt_head_w"]
    dthb = inputs["L0_attn_dt_head_b"]
    opw = inputs["L0_attn_out_proj_w"]
    Dv = inputs["L0_attn_D"]
    fw = inputs["L0_ffn_weight"]

    owt = np.ascontiguousarray(opw.T.reshape(16, 128, 1024).astype(f))
    fwt = np.ascontiguousarray(fw.T.reshape(8, 128, 1024).astype(f))
    maps = []
    for c in range(8):
        b, h = c // 2, c % 2
        sl = slice(h * 1024, (h + 1) * 1024)
        w1 = np.concatenate([ipw[:2048][sl], ipw[2048:][sl]], axis=0)
        w1t = w1.T
        w1stat = np.stack([
            np.ascontiguousarray(
                w1t[:, m * 128:(m + 1) * 128].reshape(8, 128, 128)
                .transpose(1, 0, 2).reshape(128, 1024))
            for m in range(16)]).astype(f)
        xpt = xpw[:, sl].T
        xptstat = np.ascontiguousarray(
            xpt.reshape(8, 128, 96).transpose(1, 0, 2).reshape(128, 768)).astype(f)
        dtht = np.ascontiguousarray(dthw[sl].T).astype(f)
        maps.append({
            "hid": np.ascontiguousarray(hid[b]).astype(f),
            "w1": w1stat,
            "cw": np.ascontiguousarray(cw[sl, 0, :].reshape(PC, 128, 4)).astype(f),
            "cb": np.ascontiguousarray(cb[sl].reshape(PC, 128, 1)).astype(f),
            "dv": np.ascontiguousarray(Dv[sl].reshape(PC, 128, 1)).astype(f),
            "dtb": np.ascontiguousarray(dthb[sl].reshape(PC, 128, 1)).astype(f),
            "xpt": xptstat,
            "dtht": dtht,
            "owt": owt,
            "fwt": fwt,
        })
    return maps


from concourse.bass_utils import run_bass_kernel_spmd


def kernel(**inputs):
    NT = 32
    L = NT * T
    maps = host_inputs(inputs, NT)
    nc = build(NT, debug=False)
    res = run_bass_kernel_spmd(nc, maps, core_ids=list(range(8)))
    Lh = L // 2
    out = np.empty((4, L, D), np.float32)
    for b in range(4):
        out[b, :Lh] = np.asarray(res.results[2 * b]["out"])[:Lh]
        out[b, Lh:] = np.asarray(res.results[2 * b + 1]["out"])[Lh:]
    return out


# version 04 (final)
# speedup vs baseline: 1.1825x; 1.1825x over previous
import numpy as np
from contextlib import ExitStack
import concourse.bass as bass
import concourse.mybir as mybir

F32 = mybir.dt.float32
BF16 = mybir.dt.bfloat16
AF = mybir.ActivationFunctionType
ALU = mybir.AluOpType

T = 256
SUB = T // 128
PC = 8
N = 16
NPG = 1
NGRP = N // NPG
D = 1024
EPS = 1e-5
GROUPS = [[0, 1], [2, 3], [4, 5], [6, 7]]
ENGS = ("sync", "scalar", "vector", "tensor", "gpsimd")


class Sem:
    def __init__(self, h):
        self.h = h
        self.count = 0


class Prog:
    def __init__(self):
        self.ops = {k: [] for k in ENGS}

    def emit(self, eng, fn, waits=(), inc=None, inck=1):
        ws = [(s.h, v) for s, v in waits if v > 0]
        if inc is not None:
            inc.count += inck
        self.ops[eng].append((fn, ws, (inc.h if inc else None), inck))

    def run(self, eng, e):
        for fn, waits, inch, inck in self.ops[eng]:
            for h, v in waits:
                e.wait_ge(h, v)
            ins = fn(e)
            if inch is not None:
                ins.then_inc(inch, inck)


def build(NT, debug=False):
    L = NT * T
    nc = bass.Bass("TRN2", target_bir_lowering=False)

    hid_d = nc.dram_tensor("hid", [L, D], F32, kind="ExternalInput").ap()
    w1_d = nc.dram_tensor("w1", [16, 128, 1024], F32, kind="ExternalInput").ap()
    cw_d = nc.dram_tensor("cw", [PC, 128, 4], F32, kind="ExternalInput").ap()
    cb_d = nc.dram_tensor("cb", [PC, 128, 1], F32, kind="ExternalInput").ap()
    dv_d = nc.dram_tensor("dv", [PC, 128, 1], F32, kind="ExternalInput").ap()
    dtb_d = nc.dram_tensor("dtb", [PC, 128, 1], F32, kind="ExternalInput").ap()
    xpt_d = nc.dram_tensor("xpt", [128, 768], F32, kind="ExternalInput").ap()
    dtht_d = nc.dram_tensor("dtht", [64, 1024], F32, kind="ExternalInput").ap()
    owt_d = nc.dram_tensor("owt", [16, 128, 1024], F32, kind="ExternalInput").ap()
    fwt_d = nc.dram_tensor("fwt", [8, 128, 1024], F32, kind="ExternalInput").ap()
    out_d = nc.dram_tensor("out", [L, D], F32, kind="ExternalOutput").ap()

    owbf_d = nc.dram_tensor("owbf", [16, 128, 1024], BF16, kind="Internal").ap()
    w1bf_d = nc.dram_tensor("w1bf", [16, 128, 1024], BF16, kind="Internal").ap()
    ffqd = nc.dram_tensor("ffqd", [8, 128, 1024], BF16, kind="Internal").ap()
    repsrc_d = [nc.dram_tensor(f"repsrc{i}", [49 * T], F32, kind="Internal").ap()
                for i in range(2)]
    YLEN = 8 * 128 * T // 2
    XLEN = 96 * T
    CLEN = YLEN + XLEN
    ccin_d = [nc.dram_tensor(f"ccin{i}", [CLEN], F32, kind="Internal").ap()
              for i in range(2)]
    ccout_d = [nc.dram_tensor(f"ccout{i}", [2, CLEN], F32, kind="Internal").ap()
               for i in range(2)]

    ctx = ExitStack()
    with ctx:
        def sbt(name, shape, dt=F32):
            return ctx.enter_context(nc.sbuf_tensor(name, shape, dt))

        def psum(name):
            return ctx.enter_context(nc.psum_tensor(name, [128, 512], F32))

        xpt_bf = sbt("xpt_bf", [128, 768], BF16)
        dtht_bf = sbt("dtht_bf", [64, 1024], BF16)
        ow_buf = [sbt(f"owb{i}", [128, 1024], BF16) for i in range(2)]
        w1_buf = [sbt(f"w1b{i}", [128, 1024], BF16) for i in range(2)]
        cw_sb = sbt("cw_sb", [128, PC * 4])
        cb_sb = sbt("cb_sb", [128, PC])
        dv_sb = sbt("dv_sb", [128, PC])
        ndtb_sb = sbt("ndtb_sb", [128, PC])
        ones16 = sbt("ones16", [16, 1])
        onesc = sbt("onesc", [128, 1])
        onesr = sbt("onesr", [1, 128])
        scale_sb = sbt("scale_sb", [128, 4])
        ident = sbt("ident", [128, 128], BF16)

        hid_sb = [sbt(f"hid{i}", [128, D]) for i in range(2)]
        hid_res = [sbt(f"hidr{i}", [128, D]) for i in range(2)]
        ln_sq = sbt("ln_sq", [128, D])
        stat = sbt("stat", [128, 16])
        xn_tm = [sbt(f"xntm{i}", [128, D], BF16) for i in range(2)]
        xnT = [sbt(f"xnT{i}", [128, 8 * T], BF16) for i in range(3)]
        x_raw = sbt("x_raw", [128, PC * (T + 3)])
        x_silu = [sbt(f"xsl{i}", [128, 8 * T]) for i in range(2)]
        xs_bf = sbt("xs_bf", [128, 8 * T], BF16)
        silz = [sbt(f"silz{i}", [128, 8 * T]) for i in range(2)]
        e_fl = sbt("e_fl", [128, 8 * T])
        g_fl = [sbt(f"g{i}", [128, 8 * T]) for i in range(2)]
        u_fl = [sbt(f"u{i}", [128, 8 * T]) for i in range(2)]
        xdbl = [sbt(f"xdbl{i}", [96, T]) for i in range(2)]
        xdblp = sbt("xdblp", [96, T])
        kkqk = [sbt(f"kkqk{i}", [32, T]) for i in range(2)]
        kk_sb = [sbt(f"kksb{i}", [16, T]) for i in range(2)]
        s1_sb = [sbt(f"s1sb{i}", [1, T]) for i in range(2)]
        dt_bf = [sbt(f"dtbf{i}", [64, T], BF16) for i in range(2)]
        reps = [sbt(f"reps{i}", [128, NPG * 3 * T]) for i in range(2)]
        s1rep = [sbt(f"s1rep{i}", [128, T]) for i in range(2)]
        gkk_b = [sbt(f"gkk{i}", [128, 8 * T]) for i in range(2)]
        b_fl = sbt("b_fl", [128, 8 * T])
        fix_sb = sbt("fix_sb", [128, 8])
        s_fl = sbt("s_fl", [128, 8 * T])
        y_fl = sbt("y_fl", [128, 8 * T])
        state = sbt("state", [128, 128])
        yg_bf = sbt("yg_bf", [128, 8 * T], BF16)
        ygall = [sbt(f"ygall{i}", [128, 16 * T], BF16) for i in range(1)]
        out_sb = [sbt("osb0", [128, D])]

        ps_ip = [psum("ps_ip0"), psum("ps_ip1")]
        ps_xp = psum("ps_xp")
        ps_dt = psum("ps_dt")
        ps_epi = [psum(f"ps_epi{i}") for i in range(4)]

        S = {}
        for nm in ["q_w", "s_set", "q_hid", "s_ln", "s_tr", "s_xnT", "s_ipmm",
                   "s_ipev", "s_conv", "s_xsilu", "s_xpmm", "s_xpev", "q_arst",
                   "s_ar", "q_arun", "s_kk", "s_skk", "s_s1", "q_reps", "s_dtmm",
                   "s_e", "s_den", "s_g", "s_u", "s_gb", "s_a", "s_scan",
                   "s_nblk", "s_gate", "q_agst", "s_ag", "q_agun", "s_epic",
                   "s_epimm", "s_epiev", "q_wst", "q_wst2", "q_wst3", "sv", "ss", "sg", "st", "q_stg"]:
            S[nm] = Sem(ctx.enter_context(nc.semaphore(nm)))
        for fam, nslot in [("q_w", 2), ("q_hid", 2), ("q_w1", 2), ("q_ow", 2),
                           ("q_hidres", 2), ("q_out", 2), ("q_grp", 2),
                           ("q_s1r", 2), ("q_arst", 2), ("q_arun", 2),
                           ("q_agst", 2), ("q_agun", 2)]:
            for i in range(nslot):
                nm = f"{fam}{i}"
                S[nm] = Sem(ctx.enter_context(nc.semaphore(nm)))

        wsc = [x_silu[0][:, :1024], x_silu[1][:, :1024]]
        wq1 = g_fl[0][:, :1024]
        wq2 = g_fl[1][:, :1024]
        blk = ctx.enter_context(nc.Block())
        P = Prog()

        def bc(tile_ap, nrep):
            return tile_ap.unsqueeze(1).broadcast_to(
                [tile_ap.shape[0], nrep, tile_ap.shape[1]])

        group_tok = {}
        qs_tok = {}
        cc_tok = {}
        gkk15_tok = {}
        grp_b_tok = {}
        b15_tok = {}
        stg_tok = {}
        s1_tok = {}
        hid_tok = {}
        w1_tok = {}
        ow_tok = {}
        hidr_tok = {}
        out_tok = {}
        arst_tok = {}
        arun_tok = {}
        agst_tok = {}
        agun_tok = {}

        def emit_group_load(G):
            t, g = G // N, G % N
            gp = G % 2
            qs_ = S[f"q_grp{gp}"]
            gw = [stg_tok[t]]
            if G >= 2:
                gw.append(grp_b_tok[G - 2])
                gw.append(qs_tok[G - 2])
            P.emit("scalar", (lambda t=t, g=g, gp=gp: lambda e: e.dma_start(
                out=reps[gp][:],
                in_=repsrc_d[t % 2][g * NPG * 3 * T:(g + 1) * NPG * 3 * T]
                    .unsqueeze(0).broadcast_to([128, NPG * 3 * T])))(),
                waits=gw, inc=qs_, inck=16)
            group_tok[G] = (qs_, qs_.count)

        P.emit("gpsimd", lambda e: e.memset(ident[:], 0.0), inc=S["sg"])
        _idm = S["sg"].count
        P.emit("gpsimd", lambda e: e.affine_select(
            out=ident[:], in_=ident[:], compare_op=ALU.not_equal, fill=1.0,
            base=0, pattern=[[-1, 128]], channel_multiplier=1),
            waits=[(S["sg"], _idm)], inc=S["sg"])

        P.emit("vector", lambda e: e.memset(y_fl[:], 0.0), inc=S["sv"])
        _zf = S["sv"].count
        fill_tok = {}
        for slot in range(2):
            qf = S[f"q_agst{slot}"]
            P.emit("sync", (lambda slot=slot: lambda e: e.dma_start(
                out=ccin_d[slot][0:YLEN].rearrange("(p t) -> p t", p=128),
                in_=y_fl[:, 0:YLEN // 128]))(),
                waits=[(S["sv"], _zf)], inc=qf, inck=16)
            P.emit("sync", (lambda slot=slot: lambda e: e.dma_start(
                out=ccin_d[slot][YLEN:].rearrange("(p t) -> p t", p=128),
                in_=y_fl[:, 0:XLEN // 128]))(),
                waits=[(qf, qf.count)], inc=qf, inck=16)
            fill_tok[slot] = (qf, qf.count)
        P.emit("vector", lambda e: e.memset(ones16[:], 1.0))
        P.emit("vector", lambda e: e.memset(onesc[:], 1.0))
        P.emit("vector", lambda e: e.memset(onesr[:], 1.0))
        P.emit("vector", lambda e: e.memset(scale_sb[:, 3:4], EPS))
        P.emit("vector", lambda e: e.memset(x_raw[:], 0.0))
        P.emit("vector", lambda e: e.memset(state[:], 0.0), inc=S["sv"])

        for fn in [
            lambda e: e.dma_start(out=cw_sb[:].rearrange("p (c f) -> p c f", c=PC),
                                  in_=cw_d[:].rearrange("c p f -> p c f"),
                                  allow_slow_non_contiguous=True),
            lambda e: e.dma_start(out=cb_sb[:].rearrange("p (c o) -> p c o", o=1),
                                  in_=cb_d[:].rearrange("c p o -> p c o"),
                                  allow_slow_non_contiguous=True),
            lambda e: e.dma_start(out=dv_sb[:].rearrange("p (c o) -> p c o", o=1),
                                  in_=dv_d[:].rearrange("c p o -> p c o"),
                                  allow_slow_non_contiguous=True),
            lambda e: e.dma_start(out=stat[:, 8:16].rearrange("p (c o) -> p c o", o=1),
                                  in_=dtb_d[:].rearrange("c p o -> p c o"),
                                  allow_slow_non_contiguous=True),
            lambda e: e.dma_start(out=ln_sq[:, :768], in_=xpt_d[:]),
            lambda e: e.dma_start(out=u_fl[0][:64, :1024], in_=dtht_d[:]),
        ]:
            P.emit("sync", fn, waits=[(S["q_w0"], S["q_w0"].count)],
                   inc=S["q_w0"], inck=16)
        q_small = S["q_w0"].count
        P.emit("vector", lambda e: e.tensor_scalar(
            out=ndtb_sb[:], in0=stat[:, 8:16], scalar1=-1.0, scalar2=None,
            op0=ALU.mult), waits=[(S["q_w0"], q_small)], inc=S["sv"])
        P.emit("vector", lambda e: e.tensor_copy(out=xpt_bf[:], in_=ln_sq[:, :768]),
               inc=S["sv"])
        P.emit("vector", lambda e: e.tensor_copy(out=dtht_bf[:], in_=u_fl[0][:64, :1024]),
               inc=S["sv"])
        set_small = S["sv"].count

        cast_done = {}
        absdone = {}
        w1st_done = {}
        ffqst_done = {}
        owst_done = {}
        for m in range(16):
            buf = wsc[m % 2]
            waits = [(S["ss"], cast_done[m - 2])] if m >= 2 else []
            waits.append((S["q_w0"], S["q_w0"].count))
            P.emit("sync", (lambda mm, bb: lambda e: e.dma_start(out=bb[:], in_=w1_d[mm]))(m, buf),
                   waits=waits, inc=S["q_w0"], inck=16)
            P.emit("scalar", (lambda mm, bb: lambda e: e.activation(
                out=xs_bf[:, :1024], in_=bb[:], func=AF.Identity))(m, buf),
                waits=[(S["q_w0"], S["q_w0"].count),
                       (S["q_wst"], w1st_done.get(m - 1, 0))], inc=S["ss"])
            cast_done[m] = S["ss"].count
            P.emit("scalar", (lambda mm: lambda e: e.dma_start(
                out=w1bf_d[mm], in_=xs_bf[:, :1024]))(m),
                waits=[(S["ss"], cast_done[m])], inc=S["q_wst"], inck=16)
            w1st_done[m] = S["q_wst"].count
        set_w1 = S["ss"].count

        for c in range(8):
            buf = wsc[c % 2]
            waits = ([(S["sv"], absdone[c - 2])] if c >= 2
                     else [(S["ss"], set_w1)])
            waits.append((S["q_w0"], S["q_w0"].count))
            P.emit("sync", (lambda cc, bb: lambda e: e.dma_start(out=bb[:], in_=fwt_d[cc]))(c, buf),
                   waits=waits, inc=S["q_w0"], inck=16)
            P.emit("vector", (lambda cc, bb: lambda e: e.tensor_reduce(
                out=stat[:, cc:cc + 1], in_=bb[:], axis=mybir.AxisListType.X,
                op=ALU.add, apply_absolute_value=True))(c, buf),
                waits=[(S["q_w0"], S["q_w0"].count)], inc=S["sv"])
            absdone[c] = S["sv"].count
        set_abs = S["sv"].count

        P.emit("vector", lambda e: e.tensor_reduce(
            out=stat[:, 8:9], in_=stat[:, 0:8], axis=mybir.AxisListType.X,
            op=ALU.add), waits=[(S["sv"], set_abs)], inc=S["sv"])
        P.emit("tensor", lambda e: e.matmul(out=ps_xp[0:1, 0:1], lhsT=stat[:, 8:9],
                                            rhs=onesc[:], start=True, stop=True),
               waits=[(S["sv"], S["sv"].count)], inc=S["st"])
        P.emit("vector", lambda e: e.tensor_scalar(
            out=stat[:1, 9:10], in0=ps_xp[0:1, 0:1], scalar1=1.0 / (1024.0 * 1024.0),
            scalar2=1e-5, op0=ALU.mult, op1=ALU.max),
            waits=[(S["st"], S["st"].count)], inc=S["sv"])
        P.emit("tensor", lambda e: e.matmul(out=ps_xp[:, 1:2], lhsT=onesr[:],
                                            rhs=stat[:1, 9:10], start=True, stop=True),
               waits=[(S["sv"], S["sv"].count)], inc=S["st"])
        P.emit("vector", lambda e: e.tensor_scalar(
            out=scale_sb[:, 0:1], in0=ps_xp[:, 1:2], scalar1=1.0, scalar2=None,
            op0=ALU.mult), waits=[(S["st"], S["st"].count)], inc=S["sv"])
        sc0 = S["sv"].count
        P.emit("vector", lambda e: e.tensor_scalar(
            out=scale_sb[:, 1:2], in0=scale_sb[:, 0:1], scalar1=0.5, scalar2=None,
            op0=ALU.mult), waits=[(S["sv"], sc0)], inc=S["sv"])
        P.emit("vector", lambda e: e.tensor_scalar(
            out=scale_sb[:, 2:3], in0=scale_sb[:, 0:1], scalar1=-0.5, scalar2=None,
            op0=ALU.mult), inc=S["sv"])
        P.emit("vector", lambda e: e.engine_nop(), inc=S["sv"])
        set_scale = S["sv"].count

        qdone = {}
        for c in range(8):
            buf = wsc[c % 2]
            waits = ([(S["sv"], qdone[c - 2])] if c >= 2
                     else [(S["sv"], set_scale)])
            waits.append((S["q_w0"], S["q_w0"].count))
            P.emit("sync", (lambda cc, bb: lambda e: e.dma_start(out=bb[:], in_=fwt_d[cc]))(c, buf),
                   waits=waits, inc=S["q_w0"], inck=16)
            qv = S["q_w0"].count
            P.emit("vector", (lambda cc, bb: lambda e: e.tensor_scalar(
                out=wq1[:], in0=bb[:], scalar1=scale_sb[:, 1:2], scalar2=None,
                op0=ALU.is_ge))(c, buf),
                waits=[(S["q_w0"], qv), (S["sv"], set_scale),
                       (S["sv"], qdone.get("yg", 0))], inc=S["sv"])
            P.emit("vector", (lambda cc, bb: lambda e: e.tensor_scalar(
                out=wq2[:], in0=bb[:], scalar1=scale_sb[:, 2:3], scalar2=None,
                op0=ALU.is_le))(c, buf),
                waits=[(S["sv"], qdone.get(c - 1, 0))], inc=S["sv"])
            t2v = S["sv"].count
            P.emit("vector", lambda e: e.tensor_tensor(
                out=wq1[:], in0=wq1[:], in1=wq2[:], op=ALU.subtract),
                waits=[(S["sv"], t2v)], inc=S["sv"])
            qdone[c] = S["sv"].count
            yw = [(S["q_wst2"], qdone.get("st", 0)), (S["sv"], qdone[c])]
            P.emit("vector", lambda e: e.tensor_scalar(
                out=yg_bf[:, :1024], in0=wq1[:],
                scalar1=scale_sb[:, 0:1], scalar2=None, op0=ALU.mult),
                waits=yw, inc=S["sv"])
            qdone["yg"] = S["sv"].count
            P.emit("scalar", (lambda cc: lambda e: e.dma_start(
                out=ffqd[cc], in_=yg_bf[:, :1024]))(c),
                waits=[(S["sv"], S["sv"].count)], inc=S["q_wst2"], inck=16)
            qdone["st"] = S["q_wst2"].count
            ffqst_done[c] = S["q_wst2"].count
        set_ffq = S["sv"].count

        owdone = {}
        for c in range(16):
            buf = wsc[c % 2]
            waits = [(S["ss"], owdone[c - 2])] if c >= 2 else [(S["sv"], set_ffq)]
            waits.append((S["q_w0"], S["q_w0"].count))
            P.emit("sync", (lambda cc, bb: lambda e: e.dma_start(out=bb[:], in_=owt_d[cc]))(c, buf),
                   waits=waits, inc=S["q_w0"], inck=16)
            qv = S["q_w0"].count
            P.emit("scalar", (lambda cc, bb: lambda e: e.activation(
                out=xs_bf[:, :1024], in_=bb[:], func=AF.Identity))(c, buf),
                waits=[(S["q_w0"], qv),
                       (S["q_wst3"], owst_done.get(c - 1, 0)),
                       (S["q_wst"], w1st_done[15] if c == 0 else 0)], inc=S["ss"])
            owdone[c] = S["ss"].count
            P.emit("scalar", (lambda cc: lambda e: e.dma_start(
                out=owbf_d[cc], in_=xs_bf[:, :1024]))(c),
                waits=[(S["ss"], owdone[c])], inc=S["q_wst3"], inck=16)
            owst_done[c] = S["q_wst3"].count

        SETUP_QW = S["q_w0"].count
        SETUP_QWST = S["q_wst"].count
        SETUP_QWST2 = S["q_wst2"].count
        SETUP_QWST3 = S["q_wst3"].count
        SETUP_SV = S["sv"].count
        r2_done = [0]


        def prep_ln(tp):
            for s in range(SUB):
                slot = s
                P.emit("scalar", (lambda slot=slot: lambda e: e.activation(
                    out=ln_sq[:], in_=hid_sb[slot][:], func=AF.Square))(),
                    waits=[hid_tok[(tp, s)], (S["sv"], r2_done[0])],
                    inc=S["ss"])
                sqv = S["ss"].count
                P.emit("vector", (lambda slot=slot: lambda e: e.tensor_reduce(
                    out=stat[:, 0:1], in_=hid_sb[slot][:],
                    axis=mybir.AxisListType.X, op=ALU.add))(),
                    waits=[hid_tok[(tp, s)]], inc=S["sv"])
                r1v = S["sv"].count
                P.emit("vector", lambda e: e.tensor_reduce(
                    out=stat[:, 1:2], in_=ln_sq[:], axis=mybir.AxisListType.X,
                    op=ALU.add), waits=[(S["ss"], sqv)], inc=S["sv"])
                r2_done[0] = S["sv"].count
                P.emit("vector", lambda e: e.tensor_scalar(
                    out=stat[:, 2:3], in0=stat[:, 0:1], scalar1=1.0 / D,
                    scalar2=None, op0=ALU.mult), waits=[(S["sv"], r1v)],
                    inc=S["sv"])
                v1 = S["sv"].count
                P.emit("vector", lambda e: e.tensor_scalar(
                    out=stat[:, 3:4], in0=stat[:, 1:2], scalar1=1.0 / D,
                    scalar2=None, op0=ALU.mult), waits=[(S["sv"], r2_done[0])],
                    inc=S["sv"])
                v2 = S["sv"].count
                P.emit("vector", lambda e: e.tensor_tensor(
                    out=stat[:, 4:5], in0=stat[:, 2:3], in1=stat[:, 2:3],
                    op=ALU.mult), waits=[(S["sv"], v1)], inc=S["sv"])
                v3 = S["sv"].count
                P.emit("vector", lambda e: e.tensor_tensor(
                    out=stat[:, 5:6], in0=stat[:, 3:4], in1=stat[:, 4:5],
                    op=ALU.subtract), waits=[(S["sv"], v3)], inc=S["sv"])
                varv = S["sv"].count
                P.emit("scalar", lambda e: e.activation(
                    out=stat[:, 6:7], in_=stat[:, 5:6], func=AF.Ln, bias=scale_sb[:, 3:4]),
                    waits=[(S["sv"], varv)], inc=S["ss"])
                lnv = S["ss"].count
                P.emit("scalar", lambda e: e.activation(
                    out=stat[:, 7:8], in_=stat[:, 6:7], func=AF.Exp, scale=-0.5),
                    waits=[(S["ss"], lnv)], inc=S["ss"])
                rsv = S["ss"].count
                P.emit("vector", lambda e: e.scalar_tensor_tensor(
                    out=stat[:, 8:9], in0=stat[:, 2:3], scalar=-1.0,
                    in1=stat[:, 7:8], op0=ALU.mult, op1=ALU.mult),
                    waits=[(S["ss"], rsv)], inc=S["sv"])
                ngv = S["sv"].count
                P.emit("vector", (lambda slot=slot, s=s: lambda e: e.tensor_scalar(
                    out=xn_tm[s][:], in0=hid_sb[slot][:],
                    scalar1=stat[:, 7:8], scalar2=stat[:, 8:9],
                    op0=ALU.mult, op1=ALU.add))(),
                    waits=[(S["sv"], ngv)], inc=S["s_ln"])

        def prep_transp_ip(tp):
            for t in range(16):
                s, kc = t // 8, t % 8
                bank = ps_xp if t % 2 == 0 else ps_dt
                tw = [(S["s_ln"], 2 * tp + s + 1)]
                if tp == 0 and t == 0:
                    tw.append((S["sg"], 2))
                if t >= 2:
                    tw.append((S["s_xnT"], 16 * tp + t - 1))
                if t < 2:
                    tw.append((S["s_s1"], tp))
                    tw.append((S["s_e"], 8 * tp))
                P.emit("tensor", (lambda s=s, kc=kc, bank=bank: lambda e:
                                  e.transpose(out=bank[:, 0:64].bitcast(BF16),
                                              in_=xn_tm[s][:, kc * 128:(kc + 1) * 128],
                                              identity=ident[:]))(),
                       waits=tw, inc=S["s_tr"])
                P.emit("scalar", (lambda tp=tp, s=s, kc=kc, bank=bank, t=t: lambda e:
                                  e.activation(
                    out=xnT[tp % 3][:, kc * T + s * 128:kc * T + (s + 1) * 128],
                    in_=bank[:, 0:64].bitcast(BF16), func=AF.Identity))(),
                    waits=[(S["s_tr"], 16 * tp + t + 1)], inc=S["s_xnT"])
            for m in range(16):
                bank = ps_ip[m % 2]
                for k in range(8):
                    mw = []
                    if k == 0:
                        mw.append((S["s_xnT"], 16 * (tp + 1)))
                        mw.append((S["s_ipev"], 16 * tp + m - 1))
                        mw.append(w1_tok[(tp, m)])
                    P.emit("tensor", (lambda tp=tp, m=m, k=k, bank=bank: lambda e:
                                      e.matmul(
                        out=bank[:, 0:T],
                        lhsT=w1_buf[m % 2][:, k * 128:(k + 1) * 128],
                        rhs=xnT[tp % 3][:, k * T:(k + 1) * T],
                        start=(k == 0), stop=(k == 7)))(),
                        waits=mw, inc=(S["s_ipmm"] if k == 7 else None))
                if m < 8:
                    ew = [(S["s_ipmm"], 16 * tp + m + 1), (S["s_conv"], tp)]
                    P.emit("scalar", (lambda m=m, bank=bank: lambda e: e.activation(
                        out=x_raw[:, m * (T + 3) + 3:(m + 1) * (T + 3)],
                        in_=bank[:, 0:T], func=AF.Identity))(),
                        waits=ew, inc=S["s_ipev"])
                else:
                    ew = [(S["s_ipmm"], 16 * tp + m + 1), (S["s_gate"], tp - 1)]
                    P.emit("scalar", (lambda tp=tp, m=m, bank=bank: lambda e:
                                      e.activation(
                        out=silz[tp % 2][:, (m - 8) * T:(m - 7) * T],
                        in_=bank[:, 0:T], func=AF.Silu))(),
                        waits=ew, inc=S["s_ipev"])

        def prep_conv(tp):
            cw0 = [(S["s_ipev"], 16 * tp + 8), (S["s_gate"], tp - 1)]
            if tp - 2 in b15_tok:
                cw0.append(b15_tok[tp - 2])
            if tp == 0:
                cw0.append((S["q_w0"], SETUP_QW))
                cw0.append((S["q_wst"], SETUP_QWST))
                cw0.append((S["q_wst2"], SETUP_QWST2))
                cw0.append((S["q_wst3"], SETUP_QWST3))
            xr3 = x_raw[:].rearrange("p (c t) -> p c t", t=T + 3)
            cw3 = cw_sb[:].rearrange("p (c f) -> p c f", f=4)

            def cwbc(tap):
                return cw3[:, :, tap].unsqueeze(2).broadcast_to([128, 8, T])

            xs3 = x_silu[tp % 2][:].rearrange("p (c t) -> p c t", t=T)
            u3 = u_fl[tp % 2][:].rearrange("p (c t) -> p c t", t=T)
            P.emit("vector", (lambda tp=tp, xr3=xr3: lambda e: e.tensor_tensor(
                out=xs3, in0=xr3[:, :, 0:T], in1=cwbc(0), op=ALU.mult))(),
                waits=cw0, inc=S["sv"])
            for tap in (1, 2, 3):
                tv = S["sv"].count
                P.emit("vector", (lambda tap=tap, xr3=xr3, u3=u3: lambda e:
                                  e.tensor_tensor(
                    out=u3, in0=xr3[:, :, tap:tap + T], in1=cwbc(tap),
                    op=ALU.mult))(), waits=[(S["sv"], tv)], inc=S["sv"])
                tv = S["sv"].count
                P.emit("vector", (lambda tp=tp: lambda e: e.tensor_tensor(
                    out=x_silu[tp % 2][:], in0=x_silu[tp % 2][:],
                    in1=u_fl[tp % 2][:], op=ALU.add))(),
                    waits=[(S["sv"], tv)], inc=S["sv"])
            tv = S["sv"].count
            P.emit("vector", (lambda xr3=xr3: lambda e: e.tensor_copy(
                out=xr3[:, :, 0:3], in_=xr3[:, :, T:T + 3]))(),
                waits=[(S["sv"], tv)], inc=S["sv"])
            P.emit("vector", lambda e: e.engine_nop(), inc=S["s_conv"])
            for pc in range(PC):
                P.emit("scalar", (lambda tp=tp, pc=pc: lambda e: e.activation(
                    out=x_silu[tp % 2][:, pc * T:(pc + 1) * T],
                    in_=x_silu[tp % 2][:, pc * T:(pc + 1) * T], func=AF.Silu,
                    bias=cb_sb[:, pc:pc + 1]))(),
                    waits=([(S["s_conv"], tp + 1)] if pc == 0 else []),
                    inc=(S["ss"] if pc == PC - 1 else None))
            xw = [(S["s_xpmm"], tp), (S["ss"], S["ss"].count)]
            if tp == 0:
                xw.append((S["q_wst3"], SETUP_QWST3))
            P.emit("scalar", (lambda tp=tp: lambda e: e.activation(
                out=xs_bf[:], in_=x_silu[tp % 2][:], func=AF.Identity))(),
                waits=xw, inc=S["s_xsilu"])

        def prep_xp_ar(tp):
            for k in range(8):
                xw = ([(S["s_xsilu"], tp + 1), (S["s_xnT"], 16 * (tp + 1)),
                       (S["s_s1"], tp)] if k == 0 else [])
                P.emit("tensor", (lambda k=k: lambda e: e.matmul(
                    out=ps_xp[0:96, 0:T],
                    lhsT=xpt_bf[:, k * 96:(k + 1) * 96],
                    rhs=xs_bf[:, k * T:(k + 1) * T],
                    start=(k == 0), stop=(k == 7)))(),
                    waits=xw, inc=(S["s_xpmm"] if k == 7 else None))
            P.emit("scalar", lambda e: e.activation(
                out=xdblp[:], in_=ps_xp[0:96, 0:T], func=AF.Identity),
                waits=[(S["s_xpmm"], tp + 1)], inc=S["s_xpev"])
            xpw_ = [(S["s_xpev"], tp + 1)]
            if tp - 3 in cc_tok:
                xpw_.append(cc_tok[tp - 3])
            if tp < 2:
                xpw_.append(fill_tok[tp % 2])
            P.emit("scalar", (lambda tp=tp: lambda e: e.dma_start(
                out=ccin_d[tp % 2][YLEN:].rearrange("(p t) -> p t", p=96),
                in_=xdblp[:]))(),
                waits=xpw_, inc=S[f"q_arst{tp % 2}"], inck=16)
            arst_tok[tp] = (S[f"q_arst{tp % 2}"], S[f"q_arst{tp % 2}"].count)

        def collective_cc(j):
            tp, te = j + 1, j - 1
            slot = (j + 1) % 2
            cw_ = []
            if 0 <= tp < NT:
                cw_.append(arst_tok[tp])
            if 0 <= te < NT:
                cw_.append(agst_tok[te])
            if te - 2 >= 0:
                cw_.append(agun_tok[te - 2])
            if tp - 2 >= 0 and (tp - 2) in arun_tok:
                cw_.append(arun_tok[tp - 2])
            if j <= 0:
                cw_.append(fill_tok[slot])
            P.emit("gpsimd", (lambda slot=slot: lambda e: e.collective_compute(
                "AllGather", ALU.bypass, replica_groups=GROUPS,
                ins=[ccin_d[slot][:]], outs=[ccout_d[slot][:]]))(),
                waits=cw_, inc=S["s_ag"])
            cc_tok[j] = (S["s_ag"], S["s_ag"].count)

        def prep_post_ar(tp):
            qa = S[f"q_arun{tp % 2}"]
            P.emit("scalar", (lambda tp=tp: lambda e: e.dma_start(
                out=xdbl[tp % 2][:],
                in_=ccout_d[tp % 2][0, YLEN:]
                    .rearrange("(p t) -> p t", p=96)))(),
                waits=[cc_tok[tp - 1]], inc=qa, inck=16)
            t0_ = (qa, qa.count)
            P.emit("scalar", (lambda tp=tp: lambda e: e.dma_start(
                out=xdblp[:],
                in_=ccout_d[tp % 2][1, YLEN:]
                    .rearrange("(p t) -> p t", p=96)))(),
                waits=[t0_, arst_tok[tp]], inc=qa, inck=16)
            arun_tok[tp] = (qa, qa.count)
            P.emit("vector", (lambda tp=tp: lambda e: e.tensor_tensor(
                out=xdbl[tp % 2][:], in0=xdbl[tp % 2][:], in1=xdblp[:],
                op=ALU.add))(), waits=[arun_tok[tp]], inc=S["sv"])
            xdbl_tok = (S["sv"], S["sv"].count)
            P.emit("scalar", (lambda tp=tp: lambda e: e.activation(
                out=kk_sb[tp % 2][:], in_=xdbl[tp % 2][64:80, :],
                func=AF.Square))(), waits=[xdbl_tok])
            P.emit("scalar", (lambda tp=tp: lambda e: e.activation(
                out=kkqk[tp % 2][0:32, :], in_=xdbl[tp % 2][64:96, :],
                func=AF.Identity))())
            P.emit("scalar", (lambda tp=tp: lambda e: e.activation(
                out=dt_bf[tp % 2][:], in_=xdbl[tp % 2][0:64, :],
                func=AF.Identity))(), inc=S["s_kk"])
            P.emit("tensor", (lambda tp=tp: lambda e: e.matmul(
                out=ps_xp[0:1, 256:256 + T], lhsT=ones16[:],
                rhs=kk_sb[tp % 2][:], start=True, stop=True))(),
                waits=[(S["s_kk"], tp + 1), (S["s_xpev"], tp + 1)],
                inc=S["s_skk"])
            P.emit("scalar", (lambda tp=tp: lambda e: e.activation(
                out=s1_sb[tp % 2][:], in_=ps_xp[0:1, 256:256 + T],
                func=AF.Identity, bias=1.0))(),
                waits=[(S["s_skk"], tp + 1)], inc=S["s_s1"])
            for m in range(8):
                dw = [(S["s_kk"], tp + 1), (S["s_e"], 8 * tp + m)]
                P.emit("tensor", (lambda tp=tp, m=m: lambda e: e.matmul(
                    out=ps_dt[:, 0:T],
                    lhsT=dtht_bf[:, m * 128:(m + 1) * 128],
                    rhs=dt_bf[tp % 2][:], start=True, stop=True))(),
                    waits=dw, inc=S["s_dtmm"])
                P.emit("scalar", (lambda tp=tp, m=m: lambda e: e.activation(
                    out=e_fl[:, m * T:(m + 1) * T], in_=ps_dt[:, 0:T],
                    func=AF.Exp, bias=ndtb_sb[:, m:m + 1], scale=-1.0))(),
                    waits=[(S["s_dtmm"], 8 * tp + m + 1)], inc=S["s_e"])

        def prep_reps(tp):
            srcs = [(kk_sb[tp % 2][0:16, :], 0), (kkqk[tp % 2][0:16, :], 1),
                    (kkqk[tp % 2][16:32, :], 2)]
            for i, (sap, off) in enumerate(srcs):
                sw_ = [(S["q_stg"], S["q_stg"].count)]
                if i == 0:
                    sw_.append((S["s_kk"], tp + 1))
                P.emit("scalar", (lambda sap=sap, tp=tp, off=off: lambda e:
                                  e.dma_start(
                    out=repsrc_d[tp % 2][:48 * T].rearrange(
                        "(n a t) -> n a t", n=16, a=3)[:, off, :],
                    in_=sap))(),
                    waits=sw_, inc=S["q_stg"], inck=16)
            P.emit("scalar", (lambda tp=tp: lambda e: e.dma_start(
                out=repsrc_d[tp % 2][48 * T:49 * T], in_=s1_sb[tp % 2][:]))(),
                waits=[(S["s_s1"], tp + 1), (S["q_stg"], S["q_stg"].count)],
                inc=S["q_stg"], inck=16)
            stg_tok[tp] = (S["q_stg"], S["q_stg"].count)
            qs1 = S[f"q_s1r{tp % 2}"]
            P.emit("scalar", (lambda tp=tp: lambda e: e.dma_start(
                out=s1rep[tp % 2][:],
                in_=repsrc_d[tp % 2][48 * T:49 * T].unsqueeze(0)
                    .broadcast_to([128, T])))(),
                waits=[(S["s_den"], 2 * (tp - 1)), stg_tok[tp]],
                inc=qs1, inck=16)
            s1_tok[tp] = (qs1, qs1.count)
            if tp == 0:
                emit_group_load(0)
                emit_group_load(1)

        def prep_dtv(tp):
            for h in range(2):
                dw = []
                if h == 0:
                    dw = [(S["s_e"], 8 * (tp + 1)), s1_tok[tp]]
                    if tp - 2 in gkk15_tok:
                        dw.append(gkk15_tok[tp - 2])
                P.emit("vector", (lambda tp=tp, h=h: lambda e: e.tensor_tensor(
                    out=g_fl[tp % 2][:, h * 4 * T:(h + 1) * 4 * T].rearrange(
                        "p (c t) -> p c t", t=T),
                    in0=e_fl[:, h * 4 * T:(h + 1) * 4 * T].rearrange(
                        "p (c t) -> p c t", t=T),
                    in1=bc(s1rep[tp % 2][:], 4), op=ALU.add))(),
                    waits=dw, inc=S["s_den"])
            dnv = S["s_den"].count
            P.emit("scalar", (lambda tp=tp: lambda e: e.activation(
                out=e_fl[:], in_=g_fl[tp % 2][:], func=AF.Ln))(),
                waits=[(S["s_den"], dnv)], inc=S["s_g"])
            lnv = S["s_g"].count
            P.emit("scalar", (lambda tp=tp: lambda e: e.activation(
                out=g_fl[tp % 2][:], in_=e_fl[:], func=AF.Exp, scale=-1.0))(),
                waits=[(S["s_g"], lnv)], inc=S["s_g"])

        def prep_u(tp):
            P.emit("vector", (lambda tp=tp: lambda e: e.tensor_tensor(
                out=u_fl[tp % 2][:], in0=x_silu[tp % 2][:], in1=g_fl[tp % 2][:],
                op=ALU.mult))(),
                waits=[(S["s_g"], 2 * (tp + 1))] +
                      ([b15_tok[tp - 2]] if tp - 2 in b15_tok else []),
                inc=S["s_u"])

        def scan_nblock(j, n):
            jp = j % 2
            G = N * j + n
            np_ = n % 2
            gp = G % 2
            kkr = reps[gp][:, 0 * T:1 * T]
            kr = reps[gp][:, 1 * T:2 * T]
            qr = reps[gp][:, 2 * T:3 * T]
            gw = [group_tok[G]]
            if n == 0:
                gw += [(S["s_g"], 2 * (j + 1)), (S["s_u"], j + 1)]
            if G >= 2:
                gw.append((S["s_scan"], G - 1))
            P.emit("vector", (lambda jp=jp, np_=np_, kkr=kkr: lambda e:
                              e.tensor_tensor(
                out=gkk_b[np_][:].rearrange("p (c t) -> p c t", t=T),
                in0=g_fl[jp][:].rearrange("p (c t) -> p c t", t=T),
                in1=bc(kkr, 8), op=ALU.mult))(), waits=gw, inc=S["s_gb"])
            gkk_tok = (S["s_gb"], S["s_gb"].count)
            if n == N - 1:
                gkk15_tok[j] = gkk_tok
            bw = []
            if G >= 1:
                bw.append((S["s_scan"], G))
            P.emit("vector", (lambda jp=jp, kr=kr: lambda e: e.tensor_tensor(
                out=b_fl[:].rearrange("p (c t) -> p c t", t=T),
                in0=u_fl[jp][:].rearrange("p (c t) -> p c t", t=T),
                in1=bc(kr, 8), op=ALU.mult))(), waits=bw, inc=S["s_gb"])
            b_tok = (S["s_gb"], S["s_gb"].count)
            grp_b_tok[G] = b_tok
            if n == N - 1:
                b15_tok[j] = b_tok
            P.emit("scalar", (lambda np_=np_: lambda e: e.activation(
                out=gkk_b[np_][:], in_=gkk_b[np_][:], func=AF.Identity,
                bias=1.0, scale=-1.0))(),
                waits=[gkk_tok], inc=S["s_a"])
            a_tok = (S["s_a"], S["s_a"].count)
            a3 = gkk_b[np_][:].rearrange("p (c t) -> p c t", t=T)
            b3 = b_fl[:].rearrange("p (c t) -> p c t", t=T)
            abound = a3[:, 1:8, 0]
            if j >= 1:
                stv = state[:].rearrange("p (c nn) -> p c nn", nn=16)[:, 1:8, n]
                P.emit("vector", (lambda abound=abound, stv=stv: lambda e:
                                  e.tensor_tensor(
                    out=fix_sb[:, 0:7], in0=abound, in1=stv, op=ALU.mult))(),
                    waits=[a_tok, (S["s_nblk"], 16 * (j - 1) + n + 1)],
                    inc=S["s_gb"])
                tmp_tok = (S["s_gb"], S["s_gb"].count)
                P.emit("vector", (lambda b3=b3: lambda e: e.tensor_tensor(
                    out=b3[:, 1:8, 0], in0=b3[:, 1:8, 0], in1=fix_sb[:, 0:7],
                    op=ALU.add))(), waits=[b_tok, tmp_tok], inc=S["s_gb"])
                bfix_tok = (S["s_gb"], S["s_gb"].count)
                zw = [tmp_tok]
            else:
                bfix_tok = b_tok
                zw = [a_tok]
            P.emit("vector", (lambda abound=abound: lambda e: e.tensor_scalar(
                out=abound, in0=abound, scalar1=0.0, scalar2=None,
                op0=ALU.mult))(), waits=zw, inc=S["s_gb"])
            zero_tok = (S["s_gb"], S["s_gb"].count)
            sw = [zero_tok, bfix_tok, group_tok[G],
                  (S["sv"], S["sv"].count)]
            if n == 0 and j >= 1:
                sw.append((S["s_gate"], j))
            init = 0.0 if j == 0 else state[:, n:n + 1]
            P.emit("vector", (lambda np_=np_, init=init: lambda e:
                              e.tensor_tensor_scan(
                out=s_fl[:], data0=gkk_b[np_][:], data1=b_fl[:],
                initial=init, op0=ALU.mult, op1=ALU.add))(),
                waits=sw, inc=S["s_scan"])
            P.emit("vector", (lambda n=n: lambda e: e.tensor_copy(
                out=state[:].rearrange("p (c nn) -> p c nn", nn=16)[:, :, n],
                in_=s_fl[:].rearrange("p (c t) -> p c t", t=T)[:, :, T - 1]))(),
                waits=[(S["s_scan"], G + 1)], inc=S["s_nblk"])
            P.emit("vector", (lambda qr=qr: lambda e: e.tensor_tensor(
                out=s_fl[:].rearrange("p (c t) -> p c t", t=T),
                in0=s_fl[:].rearrange("p (c t) -> p c t", t=T),
                in1=bc(qr, 8), op=ALU.mult))(),
                waits=[(S["s_nblk"], S["s_nblk"].count)], inc=S["sv"])
            qs_tok[G] = (S["sv"], S["sv"].count)
            if n == 0:
                yw_ = [qs_tok[G]]
                if j == 0:
                    yw_ += [fill_tok[0], fill_tok[1]]
                P.emit("vector", (lambda yw_=yw_: lambda e: e.tensor_copy(
                    out=y_fl[:], in_=s_fl[:]))(),
                    waits=yw_, inc=S["sv"])
            else:
                P.emit("vector", lambda e: e.tensor_tensor(
                    out=y_fl[:], in0=y_fl[:], in1=s_fl[:], op=ALU.add),
                    waits=[qs_tok[G]], inc=S["sv"])
            Gn = G + 2
            if Gn < N * NT:
                emit_group_load(Gn)

        def epi_mms(te):
            for c in range(16):
                lw = [ow_tok[(te, c)], agun_tok[te]]
                if c == 0:
                    lw.append((S["s_epiev"], 2 * te))
                for sub in range(SUB):
                    for hf in range(2):
                        P.emit("tensor", (lambda te=te, c=c, sub=sub, hf=hf:
                                          lambda e: e.matmul(
                            out=ps_epi[sub * 2 + hf][:, 0:512],
                            lhsT=ygall[0][:, c * T + sub * 128:
                                               c * T + (sub + 1) * 128],
                            rhs=ow_buf[c % 2][:, hf * 512:(hf + 1) * 512],
                            start=(c == 0), stop=False))(),
                            waits=(lw if (sub == 0 and hf == 0) else []),
                            inc=(S["s_epic"] if (sub == 1 and hf == 1)
                                 else None))
            for cc in range(8):
                fw = [ow_tok[(te, 16 + cc)]]
                for sub in range(SUB):
                    for hf in range(2):
                        P.emit("tensor", (lambda te=te, cc=cc, sub=sub, hf=hf:
                                          lambda e: e.matmul(
                            out=ps_epi[sub * 2 + hf][:, 0:512],
                            lhsT=xnT[te % 3][:, cc * T + sub * 128:
                                             cc * T + (sub + 1) * 128],
                            rhs=ow_buf[(16 + cc) % 2][:, hf * 512:(hf + 1) * 512],
                            start=False, stop=(cc == 7)))(),
                            waits=(fw if (sub == 0 and hf == 0) else []),
                            inc=(S["s_epic"] if (sub == 1 and hf == 1)
                                 else None))

        def epi_evacs(te):
            for sub in range(SUB):
                for hf in range(2):
                    ew = [(S["s_epic"], 24 * (te + 1))]
                    if hf == 0:
                        ew.append(hidr_tok[(te, sub)])
                        if sub == 1:
                            ew.append(out_tok[(te, 0)])
                        elif te >= 1:
                            ew.append(out_tok[(te - 1, 1)])
                    P.emit("vector", (lambda sub=sub, hf=hf:
                                      lambda e: e.tensor_tensor(
                        out=out_sb[0][:, hf * 512:(hf + 1) * 512],
                        in0=ps_epi[sub * 2 + hf][:, 0:512],
                        in1=hid_res[sub][:, hf * 512:(hf + 1) * 512],
                        op=ALU.add))(), waits=ew,
                        inc=(S["s_epiev"] if hf == 1 else None))
                qo = S["q_out0"]
                P.emit("sync", (lambda te=te, sub=sub: lambda e: e.dma_start(
                    out=out_d[(te * SUB + sub) * 128:
                              (te * SUB + sub + 1) * 128, :],
                    in_=out_sb[0][:]))(),
                    waits=[(S["s_epiev"], 2 * te + sub + 1),
                           (qo, qo.count)],
                    inc=qo, inck=16)
                out_tok[(te, sub)] = (qo, qo.count)

        def gate(j):
            for pc in range(PC):
                gw = ([(S["s_ipev"], 16 * (j + 1)),
                       (S["sv"], S["sv"].count)] if pc == 0 else [])
                P.emit("vector", (lambda j=j, pc=pc: lambda e:
                                  e.scalar_tensor_tensor(
                    out=s_fl[:, pc * T:(pc + 1) * T],
                    in0=x_silu[j % 2][:, pc * T:(pc + 1) * T],
                    scalar=dv_sb[:, pc:pc + 1],
                    in1=y_fl[:, pc * T:(pc + 1) * T],
                    op0=ALU.mult, op1=ALU.add))(), waits=gw,
                    inc=(S["sv"] if pc == PC - 1 else None))
            P.emit("vector", (lambda j=j: lambda e: e.tensor_tensor(
                out=yg_bf[:], in0=s_fl[:], in1=silz[j % 2][:], op=ALU.mult))(),
                waits=([agst_tok[j - 1]] if j >= 1 else []) +
                      [(S["sv"], S["sv"].count)],
                inc=S["s_gate"])
            gsw = [(S["s_gate"], j + 1)]
            if j - 1 in cc_tok:
                gsw.append(cc_tok[j - 1])
            P.emit("scalar", (lambda j=j: lambda e: e.dma_start(
                out=ccin_d[j % 2][0:YLEN].bitcast(BF16).rearrange(
                    "(c p t) -> p c t", p=128, t=T),
                in_=yg_bf[:].rearrange("p (c t) -> p c t", c=PC)))(),
                waits=gsw, inc=S[f"q_agst{j % 2}"], inck=16)
            agst_tok[j] = (S[f"q_agst{j % 2}"], S[f"q_agst{j % 2}"].count)

        def emit_agun(te):
            uw = [cc_tok[te + 1], (S["s_epic"], 24 * te)]
            qag = S[f"q_agun{te % 2}"]
            for r in range(2):
                P.emit("scalar", (lambda te=te, r=r: lambda e: e.dma_start(
                    out=ygall[0][:, r * 8 * T:(r + 1) * 8 * T].rearrange(
                        "p (c t) -> p c t", t=T),
                    in_=ccout_d[te % 2][r, 0:YLEN].bitcast(BF16).rearrange(
                        "(c p t) -> p c t", p=128, t=T)))(),
                    waits=(uw if r == 0 else [(qag, qag.count)]),
                    inc=qag, inck=16)
            agun_tok[te] = (qag, qag.count)

        for j in range(-1, NT + 1):
            tp, te = j + 1, j - 1
            do_scan = 0 <= j < NT
            do_prep = 0 <= tp < NT
            do_epi = 0 <= te < NT

            if do_prep:
                for s in range(SUB):
                    hw = []
                    if tp >= 1:
                        hw.append((S["s_ln"], 2 * (tp - 1) + s + 1))
                    qh = S[f"q_hid{s}"]
                    P.emit("sync", (lambda tp=tp, s=s: lambda e:
                                    e.dma_start(
                        out=hid_sb[s][:],
                        in_=hid_d[(tp * SUB + s) * 128:(tp * SUB + s + 1) * 128, :]))(),
                        waits=hw, inc=qh, inck=16)
                    hid_tok[(tp, s)] = (qh, qh.count)
                for m in range(16):
                    ww = [(S["q_wst"], w1st_done[m])]
                    if 16 * tp + m >= 2:
                        ww.append((S["s_ipmm"], 16 * tp + m - 1))
                    qw1 = S[f"q_w1{m % 2}"]
                    P.emit("sync", (lambda m=m: lambda e: e.dma_start(
                        out=w1_buf[m % 2][:], in_=w1bf_d[m]))(),
                        waits=ww, inc=qw1, inck=16)
                    w1_tok[(tp, m)] = (qw1, qw1.count)
            if do_epi:
                for c in range(24):
                    ow = [(S["q_wst3"], owst_done[c]) if c < 16 else (S["q_wst2"], ffqst_done[c - 16])]
                    if 24 * te + c >= 2:
                        ow.append((S["s_epic"], 24 * te + c - 1))
                    src_d = owbf_d[c] if c < 16 else ffqd[c - 16]
                    qow = S[f"q_ow{c % 2}"]
                    P.emit("sync", (lambda c=c, src_d=src_d: lambda e: e.dma_start(
                        out=ow_buf[c % 2][:], in_=src_d))(),
                        waits=ow, inc=qow, inck=16)
                    ow_tok[(te, c)] = (qow, qow.count)
                for s in range(SUB):
                    hw = []
                    if te >= 1:
                        hw.append((S["s_epiev"], 2 * (te - 1) + s + 1))
                    qhr = S[f"q_hidres{s}"]
                    P.emit("sync", (lambda te=te, s=s: lambda e: e.dma_start(
                        out=hid_res[s][:],
                        in_=hid_d[(te * SUB + s) * 128:(te * SUB + s + 1) * 128, :]))(),
                        waits=hw, inc=qhr, inck=16)
                    hidr_tok[(te, s)] = (qhr, qhr.count)

            inserts = {
                1: (lambda: prep_ln(tp)) if do_prep else None,
                2: (lambda: prep_transp_ip(tp)) if do_prep else None,
                5: (lambda: prep_conv(tp)) if do_prep else None,
                6: (lambda: prep_xp_ar(tp)) if do_prep else None,
                7: (lambda: (collective_cc(j) if (do_prep or do_epi) else None,
                             emit_agun(te) if do_epi else None,
                             epi_mms(te) if do_epi else None)),
                8: (lambda: epi_evacs(te)) if do_epi else None,
                10: (lambda: prep_post_ar(tp)) if do_prep else None,
                11: (lambda: prep_reps(tp)) if do_prep else None,
                13: (lambda: prep_dtv(tp)) if do_prep else None,
                14: (lambda: prep_u(tp)) if do_prep else None,
            }
            if do_scan:
                for n in range(N):
                    scan_nblock(j, n)
                    if inserts.get(n + 1) is not None:
                        inserts[n + 1]()
                        inserts[n + 1] = None
                gate(j)
            else:
                for key in sorted(k for k in inserts if inserts[k] is not None):
                    inserts[key]()


        P.emit("sync", lambda e: e.nop(),
               waits=[(S["q_out0"], S["q_out0"].count),
                      (S["q_out1"], S["q_out1"].count)])

        dbg_d = {}
        if debug:
            taps = [("hid0", hid_sb[0]), ("hid1", hid_sb[1]),
                    ("xntm0", xn_tm[0]), ("xntm1", xn_tm[1]),
                    ("statx", stat), ("xnt0", xnT[0]), ("xraw0", x_raw), ("xsilu0", x_silu[0]),
                    ("silz0", silz[0]), ("e0", e_fl), ("g0", g_fl[0]),
                    ("u0", u_fl[0]), ("y0", y_fl), ("state0", state),
                    ("s1rep0", s1rep[0]), ("xdbl0", xdbl[0]),
                    ("yg0", yg_bf), ("ygall0", ygall[0])]
            for nm, buf in taps:
                d = nc.dram_tensor("dbg_" + nm, list(buf.shape), buf.dtype,
                                   kind="ExternalOutput").ap()
                dbg_d[nm] = d
                P.emit("sync", (lambda d=d, buf=buf: lambda e: e.dma_start(
                    out=d, in_=buf[:]))(),
                    waits=[(S["q_out0"], S["q_out0"].count)],
                    inc=S["q_out0"], inck=16)
            P.emit("sync", lambda e: e.nop(),
                   waits=[(S["q_out0"], S["q_out0"].count)])

        @blk.sync
        def _(e):
            P.run("sync", e)

        @blk.scalar
        def _(e):
            P.run("scalar", e)

        @blk.vector
        def _(e):
            P.run("vector", e)

        @blk.tensor
        def _(e):
            P.run("tensor", e)

        @blk.gpsimd
        def _(e):
            P.run("gpsimd", e)

    return nc


def host_inputs(inputs, NT=32):
    L = NT * T
    f = np.float32
    hid = inputs["L0_attn_hidden_states"][:, :L]
    ipw = inputs["L0_attn_in_proj_w"]
    cw = inputs["L0_attn_conv_w"]
    cb = inputs["L0_attn_conv_b"]
    xpw = inputs["L0_attn_x_proj_w"]
    dthw = inputs["L0_attn_dt_head_w"]
    dthb = inputs["L0_attn_dt_head_b"]
    opw = inputs["L0_attn_out_proj_w"]
    Dv = inputs["L0_attn_D"]
    fw = inputs["L0_ffn_weight"]

    owt = np.ascontiguousarray(opw.T.reshape(16, 128, 1024).astype(f))
    fwt = np.ascontiguousarray(fw.T.reshape(8, 128, 1024).astype(f))
    maps = []
    for c in range(8):
        b, h = c // 2, c % 2
        sl = slice(h * 1024, (h + 1) * 1024)
        w1 = np.concatenate([ipw[:2048][sl], ipw[2048:][sl]], axis=0)
        w1t = w1.T
        w1stat = np.stack([
            np.ascontiguousarray(
                w1t[:, m * 128:(m + 1) * 128].reshape(8, 128, 128)
                .transpose(1, 0, 2).reshape(128, 1024))
            for m in range(16)]).astype(f)
        xpt = xpw[:, sl].T
        xptstat = np.ascontiguousarray(
            xpt.reshape(8, 128, 96).transpose(1, 0, 2).reshape(128, 768)).astype(f)
        dtht = np.ascontiguousarray(dthw[sl].T).astype(f)
        maps.append({
            "hid": np.ascontiguousarray(hid[b]).astype(f),
            "w1": w1stat,
            "cw": np.ascontiguousarray(cw[sl, 0, :].reshape(PC, 128, 4)).astype(f),
            "cb": np.ascontiguousarray(cb[sl].reshape(PC, 128, 1)).astype(f),
            "dv": np.ascontiguousarray(Dv[sl].reshape(PC, 128, 1)).astype(f),
            "dtb": np.ascontiguousarray(dthb[sl].reshape(PC, 128, 1)).astype(f),
            "xpt": xptstat,
            "dtht": dtht,
            "owt": owt,
            "fwt": fwt,
        })
    return maps


from concourse.bass_utils import run_bass_kernel_spmd


def kernel(**inputs):
    NT = 32
    L = NT * T
    maps = host_inputs(inputs, NT)
    nc = build(NT, debug=False)
    res = run_bass_kernel_spmd(nc, maps, core_ids=list(range(8)))
    Lh = L // 2
    out = np.empty((4, L, D), np.float32)
    for b in range(4):
        out[b, :Lh] = np.asarray(res.results[2 * b]["out"])[:Lh]
        out[b, Lh:] = np.asarray(res.results[2 * b + 1]["out"])[Lh:]
    return out
